# version 21
# baseline (speedup 1.0000x reference)
import contextlib
import numpy as np
import ml_dtypes
import concourse.bass as bass
import concourse.mybir as mybir
from concourse.bass_utils import run_bass_kernel_spmd

F32 = mybir.dt.float32
BF16 = mybir.dt.bfloat16
I32 = mybir.dt.int32
BF = ml_dtypes.bfloat16
Alu = mybir.AluOpType
Act = mybir.ActivationFunctionType
AX = mybir.AxisListType.X

S = 2048
D = 2048
DL = 512
NH = 16
HD = 128
DF = 8192
NT = S // 128
NFC = D // 128
NQ = 4
TQ = S // NQ

MAGIC = float(3 << 22)
EXPMASK = 0x7F800000
EMIN = float(7 << 23)
ESUB = float(6 << 23)
EINV = float(254 << 23)
SM_SCALE = float(1.0 / np.sqrt(128.0))

ENGINES = ("sync", "scalar", "vector", "tensor", "gpsimd")


class Sched:
    def __init__(self):
        self.ops = {e: [] for e in ENGINES}
        self.cnt = {}

    def bump(self, sem, amt):
        self.cnt[sem] = self.cnt.get(sem, 0) + amt
        return self.cnt[sem]

    def val(self, sem):
        return self.cnt.get(sem, 0)

    def ms(self, sem):
        return (sem, self.val(sem))

    def emit(self, eng, fn, waits=(), inc=None, inc_amt=1):
        self.ops[eng].append((tuple(waits), fn, inc, inc_amt))
        if inc:
            self.bump(inc, inc_amt)
        return self.ms(inc) if inc else None

    semnames = ()

    def barrier(self):
        return [self.ms(s) for s in self.semnames if self.val(s) > 0]

    def replay(self, eng_name, e, sems):
        waited = {}
        for waits, fn, inc, inc_amt in self.ops[eng_name]:
            for sem, v in waits:
                if v > 0 and waited.get(sem, -1) < v:
                    e.wait_ge(sems[sem], v)
                    waited[sem] = v
            last = fn(e)
            if inc is not None and last is not None:
                last.then_inc(sems[inc], inc_amt)


def quant_chain(e, src_bf, dst_bf, am, sb, ib, scb, ivb, nblk, u_tmp):
    n = nblk * 32
    src3 = src_bf[:, 0:n].rearrange("p (nb b) -> p nb b", b=32)
    e.reduce_max(out=am[:, 0:nblk], in_=src3, axis=AX, apply_absolute_value=True)
    e.drain()
    e.tensor_scalar(out=sb[:, 0:nblk], in0=am[:, 0:nblk].bitcast(I32),
                    scalar1=EXPMASK, scalar2=None, op0=Alu.bitwise_and)
    e.drain()
    e.tensor_scalar(out=sb[:, 0:nblk], in0=sb[:, 0:nblk], scalar1=EMIN, scalar2=ESUB,
                    op0=Alu.max, op1=Alu.subtract)
    e.drain()
    e.tensor_scalar(out=ib[:, 0:nblk], in0=sb[:, 0:nblk], scalar1=EINV, scalar2=-1.0,
                    op0=Alu.subtract, op1=Alu.mult)
    e.drain()
    e.tensor_copy(out=scb[:, 0:nblk], in_=sb[:, 0:nblk].bitcast(F32))
    e.drain()
    e.tensor_copy(out=ivb[:, 0:nblk], in_=ib[:, 0:nblk].bitcast(F32))
    e.drain()
    e.tensor_tensor(out=u_tmp[:, 0:n].rearrange("p (nb b) -> p nb b", b=32),
                    in0=src3,
                    in1=ivb[:, 0:nblk].unsqueeze(2).broadcast_to([128, nblk, 32]),
                    op=Alu.mult)
    e.drain()
    e.tensor_scalar(out=u_tmp[:, 0:n], in0=u_tmp[:, 0:n], scalar1=MAGIC, scalar2=MAGIC,
                    op0=Alu.add, op1=Alu.subtract)
    e.drain()
    return e.scalar_tensor_tensor(
        out=dst_bf[:, 0:n].rearrange("p (nb b) -> p nb b", b=32),
        in0=u_tmp[:, 0:n].rearrange("p (nb b) -> p nb b", b=32),
        scalar=127.0,
        in1=scb[:, 0:nblk].unsqueeze(2).broadcast_to([128, nblk, 32]),
        op0=Alu.min, op1=Alu.mult)


def build_nc():
    nc = bass.Bass("TRN2", target_bir_lowering=False)
    din = {}

    def DI(name, shape, dt=F32):
        din[name] = nc.dram_tensor(name, list(shape), dt, kind="ExternalInput").ap()
        return din[name]

    x_nat = DI("x_nat", [S, D])
    xT = DI("xT", [D, S])
    w_kv = DI("w_kv", [D, DL]); b_kv = DI("b_kv", [1, DL])
    w_ku = DI("w_ku", [DL, D // 2]); b_ku = DI("b_ku", [1, D // 2])
    w_kr = DI("w_kr", [D, D // 2]); b_kr = DI("b_kr", [1, D // 2])
    w_vu = DI("w_vu", [DL, D]); b_vu = DI("b_vu", [1, D])
    w_q = DI("w_q", [D, DL]); b_q = DI("b_q", [1, DL])
    w_qu = DI("w_qu", [DL, D // 2]); b_qu = DI("b_qu", [1, D // 2])
    w_qr = DI("w_qr", [DL, D // 2]); b_qr = DI("b_qr", [1, D // 2])
    woR = DI("woR", [128, NFC * D]); b_o = DI("b_o", [1, D])
    ln_w = DI("ln_w", [1, D]); ln_b = DI("ln_b", [1, D])
    W1 = DI("W1", [DF, D]); W1T = DI("W1T", [D, DF]); b1 = DI("b1", [1, DF])
    W2 = DI("W2", [D, DF]); W2T = DI("W2T", [DF, D]); b2 = DI("b2", [1, D])
    ident = DI("ident", [128, 128], BF16)
    ones_bf = DI("ones_bf", [128, 128], BF16)
    maskslab = DI("maskslab", [128, 2048], BF16)
    ident_f = DI("ident_f", [128, 128], F32)
    ones_f = DI("ones_f", [128, 128], F32)

    outT = nc.dram_tensor("outT", [D, S], F32, kind="ExternalOutput").ap()

    def DSC(name, shape, dt):
        return nc.dram_tensor(name, list(shape), dt, kind="Internal").ap()

    xbT_s = DSC("xbT_s", [D, S], BF16)
    hq0T_s = DSC("hq0T_s", [D, S], BF16)
    V_s = DSC("V_s", [S, D], BF16)
    acT_s = DSC("acT_s", [D, S], BF16)
    ffoT_s = DSC("ffoT_s", [D, S], BF16)
    sc1T_s = DSC("sc1T_s", [64, DF], BF16)
    iv1T_s = DSC("iv1T_s", [64, DF], BF16)
    sc2T_s = DSC("sc2T_s", [256, D], BF16)
    iv2T_s = DSC("iv2T_s", [256, D], BF16)
    W1R_s = DSC("W1R_s", [128, NFC * DF], BF16)
    W2R_s = DSC("W2R_s", [128, 64 * D], BF16)

    sch = Sched()
    ctx = contextlib.ExitStack()
    SEM_NAMES = ("eP", "eV", "eS", "dC", "dL",
                 "d0", "d1", "d2", "d3", "d4", "d5", "d6", "d7",
                 "gA", "gB", "gC", "dW0", "dW1", "dW2",
                 "dS1a", "dS1b", "dS2", "dS2q", "dS3", "dS4", "dS4b", "dS5",
                 "dW3", "dT0", "dT1", "dS2qb", "dS2b", "g5a", "g5b",
                 "dOUTa", "dOUTb")
    sems = {s_: ctx.enter_context(nc.semaphore(s_)) for s_ in SEM_NAMES}
    sch.semnames = SEM_NAMES

    def dq(e, q, out, in_):
        e.dma_start(out=out, in_=in_).then_inc(sems[q], 16)

    def w_chunked(wsrc, ncols, nchunk):
        return bass.AP(wsrc.tensor, 0,
                       [[ncols, 128], [128 * ncols, nchunk], [1, ncols]])

    def SB(st, name, shape, dt):
        return st.enter_context(nc.sbuf_tensor(name, list(shape), dt))

    def PS(st, name, shape, dt=F32):
        return st.enter_context(nc.psum_tensor(name, list(shape), dt))

    ident_t = SB(ctx, "ident_t", [128, 128], BF16)
    ones_t = SB(ctx, "ones_t", [128, 128], BF16)
    identf_t = SB(ctx, "identf_t", [128, 128], F32)
    onesf_t = SB(ctx, "onesf_t", [128, 128], F32)
    bias_o_t = SB(ctx, "bias_o_t", [128, 16], F32)
    eps_t = SB(ctx, "eps_t", [128, 1], F32)
    bias_b2_t = SB(ctx, "bias_b2_t", [128, 16], F32)

    def bias_cols(e, dst, src, n):
        return e.dma_start(out=dst[:, 0:n],
                           in_=bass.AP(src.tensor, 0, [[1, 128], [128, n]]),
                           allow_slow_non_contiguous=True)

    def g_const(e):
        dq(e, "dC", ident_t[:, :], ident[:, :])
        dq(e, "dC", ones_t[:, :], ones_bf[:, :])
        dq(e, "dC", identf_t[:, :], ident_f[:, :])
        dq(e, "dC", onesf_t[:, :], ones_f[:, :])
        bias_cols(e, bias_o_t, b_o, 16).then_inc(sems["dC"], 16)
        bias_cols(e, bias_b2_t, b2, 16).then_inc(sems["dC"], 16)
        return None
    sch.emit("sync", g_const, inc="dC", inc_amt=16 * 6)
    sch.emit("vector", lambda e: (e.memset(eps_t[:, :], 1e-5), e.drain())[-1],
             inc="eV")
    MEMS_G = None
    BAR = sch.barrier()

    s1 = contextlib.ExitStack()
    x_b = [SB(s1, f"s1_x{i}", [128, D], F32) for i in range(2)]
    xb_b = [SB(s1, f"s1_xb{i}", [128, D], BF16) for i in range(2)]
    sq_b = SB(s1, "s1_sq", [128, D], F32)
    sq2_b = SB(s1, "s1_sq2", [128, D], F32)
    z0_pp = SB(s1, "s1_z0", [128, 1], F32)
    d_b = SB(s1, "s1_d", [128, D], F32)
    t1_b = SB(s1, "s1_t1", [128, D], F32)
    h0_b = SB(s1, "s1_h0", [128, D], BF16)
    u_b = SB(s1, "s1_u", [128, D], BF16)
    hq0_b = SB(s1, "s1_hq0", [128, D], BF16)
    trev_b = [SB(s1, f"s1_tr{i}", [128, 512], BF16) for i in range(2)]
    lnw_rep = SB(s1, "s1_lnw", [128, D], F32)
    lnb_rep = SB(s1, "s1_lnb", [128, D], F32)
    st_b = SB(s1, "s1_st", [128, 32], F32)
    am_b = SB(s1, "s1_am", [128, 64], F32)
    sbts = SB(s1, "s1_sbts", [128, 64], I32)
    ibts = SB(s1, "s1_ibts", [128, 64], I32)
    scb_b = SB(s1, "s1_scb", [128, 64], BF16)
    ivb_b = SB(s1, "s1_ivb", [128, 64], BF16)
    p_tr = PS(s1, "s1_ptr", [128, 512], BF16)

    sch.emit("sync", lambda e: (
        dq(e, "dL", lnw_rep[:, :], ln_w.broadcast_to([128, D])),
        dq(e, "dL", lnb_rep[:, :], ln_b.broadcast_to([128, D])))[-1] and None,
        waits=BAR, inc="dL", inc_amt=32)
    LNREP = sch.ms("dL")

    sch.emit("vector", lambda e: (e.memset(z0_pp[:, :], 0.0),
                                  e.memset(st_b[:, :], 0.0), e.drain())[-1],
             inc="eV")
    Z0PP = sch.ms("eV")
    v_done_tile = {}
    g_done_tile = {}
    for i in range(NT):
        xb = x_b[i % 2]
        xbb = xb_b[i % 2]
        w_reuse = [v_done_tile[i - 2]] if i >= 2 else []
        slot = "d%d" % (i % 2)
        sch.emit("sync", (lambda xb=xb, i=i: lambda e:
                 e.dma_start(out=xb[:, :], in_=x_nat[i * 128:(i + 1) * 128, :]))(),
                 waits=w_reuse + (BAR if i < 2 else []), inc=slot, inc_amt=16)
        LD = sch.ms(slot)

        soff = (i % 2) * 16

        def stc(a, b):
            return st_b[:, soff + a:soff + b]

        def sums1(e, xb=xb, stc=stc):
            e.tensor_scalar(out=d_b[:, 0:D], in0=xb[:, :], scalar1=1.0,
                            scalar2=None, op0=Alu.mult, op1=Alu.add,
                            accum_out=stc(0, 1))
            e.drain()
            e.scalar_tensor_tensor(out=sq_b[:, :], in0=xb[:, :], scalar=1.0,
                                   in1=xb[:, :], op0=Alu.mult, op1=Alu.mult,
                                   accum_out=stc(1, 2))
            return e.drain()
        sch.emit("vector", sums1, waits=[LD, Z0PP], inc="eV")
        SUMS1 = sch.ms("eV")

        def v1(e, stc=stc):
            e.tensor_scalar(out=stc(2, 4), in0=stc(0, 2), scalar1=float(1.0 / D),
                            scalar2=None, op0=Alu.mult)
            e.drain()
            e.tensor_tensor(out=stc(4, 5), in0=stc(2, 3), in1=stc(2, 3),
                            op=Alu.mult)
            e.drain()
            e.tensor_tensor(out=stc(5, 6), in0=stc(3, 4), in1=stc(4, 5),
                            op=Alu.subtract)
            e.drain()
            e.tensor_scalar(out=stc(5, 6), in0=stc(5, 6),
                            scalar1=1e-5, scalar2=None, op0=Alu.add)
            return e.drain()
        sch.emit("vector", v1, waits=[SUMS1], inc="eV")
        ST1 = sch.ms("eV")
        sch.emit("scalar", (lambda stc=stc: lambda e:
                 e.activation(out=stc(6, 7), in_=stc(5, 6), func=Act.Sqrt))(),
                 waits=[ST1], inc="eS")
        SD = sch.ms("eS")

        def v2(e, xb=xb, xbb=xbb, stc=stc):
            e.reciprocal(out=stc(7, 8), in_=stc(6, 7))
            e.drain()
            e.tensor_tensor(out=stc(6, 7), in0=stc(7, 8), in1=stc(7, 8),
                            op=Alu.mult)
            e.drain()
            e.tensor_tensor(out=stc(6, 7), in0=stc(6, 7), in1=stc(5, 6),
                            op=Alu.mult)
            e.drain()
            e.tensor_scalar(out=stc(6, 7), in0=stc(6, 7),
                            scalar1=-0.5, scalar2=1.5, op0=Alu.mult, op1=Alu.add)
            e.drain()
            e.tensor_tensor(out=stc(7, 8), in0=stc(7, 8), in1=stc(6, 7),
                            op=Alu.mult)
            e.drain()
            return e.tensor_scalar(out=xbb[:, :], in0=xb[:, :],
                                   scalar1=stc(2, 3), scalar2=stc(7, 8),
                                   op0=Alu.subtract, op1=Alu.mult)
        w_v2 = [SD]
        if sch.cnt.get(("_s1_xbb_free", i - 2)) is not None:
            w_v2.append(sch.cnt[("_s1_xbb_free", i - 2)])
        sch.emit("vector", v2, waits=w_v2, inc="eV")
        XBB = sch.ms("eV")
        def sums2(e, xbb=xbb, stc=stc):
            e.tensor_scalar(out=d_b[:, 0:D], in0=xbb[:, :], scalar1=1.0,
                            scalar2=None, op0=Alu.mult, op1=Alu.add,
                            accum_out=stc(8, 9))
            e.drain()
            e.scalar_tensor_tensor(out=sq_b[:, :], in0=xbb[:, :], scalar=1.0,
                                   in1=xbb[:, :], op0=Alu.mult, op1=Alu.mult,
                                   accum_out=stc(9, 10))
            return e.drain()
        sch.emit("vector", sums2, waits=[XBB], inc="eV")
        SUMS2 = sch.ms("eV")

        def v2b(e, stc=stc):
            e.tensor_scalar(out=stc(10, 12), in0=stc(8, 10),
                            scalar1=float(1.0 / D), scalar2=None, op0=Alu.mult)
            e.drain()
            e.tensor_tensor(out=stc(12, 13), in0=stc(10, 11), in1=stc(10, 11),
                            op=Alu.mult)
            e.drain()
            e.tensor_tensor(out=stc(13, 14), in0=stc(11, 12), in1=stc(12, 13),
                            op=Alu.subtract)
            e.drain()
            e.tensor_scalar(out=stc(13, 14), in0=stc(13, 14),
                            scalar1=1e-5, scalar2=None, op0=Alu.add)
            return e.drain()
        sch.emit("vector", v2b, waits=[SUMS2], inc="eV")
        ST2 = sch.ms("eV")
        sch.emit("scalar", (lambda stc=stc: lambda e:
                 e.activation(out=stc(14, 15), in_=stc(13, 14), func=Act.Sqrt))(),
                 waits=[ST2], inc="eS")
        SD2 = sch.ms("eS")

        def v3(e, xbb=xbb, stc=stc):
            e.reciprocal(out=stc(15, 16), in_=stc(14, 15))
            e.drain()
            e.tensor_tensor(out=stc(14, 15), in0=stc(15, 16),
                            in1=stc(15, 16), op=Alu.mult)
            e.drain()
            e.tensor_tensor(out=stc(14, 15), in0=stc(14, 15),
                            in1=stc(13, 14), op=Alu.mult)
            e.drain()
            e.tensor_scalar(out=stc(14, 15), in0=stc(14, 15),
                            scalar1=-0.5, scalar2=1.5, op0=Alu.mult, op1=Alu.add)
            e.drain()
            e.tensor_scalar(out=stc(15, 16), in0=stc(15, 16),
                            scalar1=stc(14, 15), scalar2=None, op0=Alu.mult)
            e.drain()
            e.tensor_scalar(out=d_b[:, :], in0=xbb[:, :], scalar1=stc(10, 11),
                            scalar2=stc(15, 16), op0=Alu.subtract, op1=Alu.mult)
            e.drain()
            e.tensor_tensor(out=t1_b[:, :], in0=d_b[:, :], in1=lnw_rep[:, :],
                            op=Alu.mult)
            e.drain()
            e.tensor_tensor(out=h0_b[:, :], in0=t1_b[:, :], in1=lnb_rep[:, :],
                            op=Alu.add)
            e.drain()
            return quant_chain(e, h0_b, hq0_b, am_b, sbts, ibts, scb_b, ivb_b, 64, u_b)
        w_v3 = [SD2, LNREP]
        if sch.cnt.get(("_s1_xbb_free", i - 1)) is not None:
            w_v3.append(sch.cnt[("_s1_xbb_free", i - 1)])
        V3 = sch.emit("vector", v3, waits=w_v3, inc="eV")
        v_done_tile[i] = V3

        for (srcname, src, dst) in (("xb", xbb, xbT_s), ("hq0", hq0_b, hq0T_s)):
            for cg in range(4):
                trb = trev_b[cg % 2]
                w_tr = [V3]
                if sch.cnt.get("_s1_last_ev") is not None:
                    w_tr.append(sch.cnt["_s1_last_ev"])

                def mk_tr(src=src, cg=cg):
                    def f(e):
                        last = None
                        for k in range(4):
                            c = cg * 4 + k
                            last = e.transpose(out=p_tr[:, k * 128:(k + 1) * 128],
                                               in_=src[:, c * 128:(c + 1) * 128],
                                               identity=ident_t[:, :])
                        return last
                    return f
                sch.emit("tensor", mk_tr(), waits=w_tr, inc="eP")
                TR = sch.ms("eP")
                w_ev = [TR]
                if sch.cnt.get(("_s1_tr_g", cg % 2)) is not None:
                    w_ev.append(sch.cnt[("_s1_tr_g", cg % 2)])
                def mk_ev(trb=trb):
                    return lambda e: e.copy(out=trb[:, :], in_=p_tr[:, :])
                sch.emit("scalar", mk_ev(), waits=w_ev, inc="eS")
                EV = sch.ms("eS")
                sch.cnt["_s1_last_ev"] = EV

                sslot = "dS1a" if cg % 2 == 0 else "dS1b"
                def mk_st(dst=dst, i=i, cg=cg, trb=trb, sslot=sslot):
                    def f(e):
                        for k in range(4):
                            c = cg * 4 + k
                            dq(e, sslot,
                               dst[c * 128:(c + 1) * 128, i * 128:(i + 1) * 128],
                               trb[:, k * 128:(k + 1) * 128])
                        return None
                    return f
                G = sch.emit("gpsimd", mk_st(), waits=[EV], inc=sslot, inc_amt=64)
                g_done_tile[(i, srcname, cg)] = G
                sch.cnt[("_s1_tr_g", cg % 2)] = G
        sch.cnt[("_s1_xbb_free", i)] = sch.ms("eP")
    s1.close()
    BAR = sch.barrier()

    s2_units = []

    def author_s2_alloc(st):
        g = {}
        g["wnat"] = [SB(st, f"s2_wn{i}", [128, 2048], F32) for i in range(2)]
        g["am2"] = [SB(st, f"s2_am{j}", [128, 128], F32) for j in range(2)]
        g["sb2"] = [SB(st, f"s2_sb{j}", [128, 128], I32) for j in range(2)]
        g["ib2"] = [SB(st, f"s2_ib{j}", [128, 128], I32) for j in range(2)]
        g["scb2"] = [SB(st, f"s2_scb{j}", [128, 128], BF16) for j in range(2)]
        g["ivb2"] = [SB(st, f"s2_ivb{j}", [128, 128], BF16) for j in range(2)]
        g["tra"] = [SB(st, f"s2_tra{j}", [128, 128], BF16) for j in range(2)]
        g["trb"] = [SB(st, f"s2_trb{j}", [128, 128], BF16) for j in range(2)]
        g["wt"] = [SB(st, f"s2_wt{i}", [128, 2048], F32) for i in range(2)]
        g["ive"] = [SB(st, f"s2_ive{i}", [128, 2048], BF16) for i in range(2)]
        g["sce"] = [SB(st, f"s2_sce{i}", [128, 2048], BF16) for i in range(2)]
        g["uq"] = SB(st, "s2_uq", [128, 2048], F32)
        g["rq"] = SB(st, "s2_rq", [128, 2048], BF16)
        g["wq"] = [SB(st, f"s2_wq{i}", [128, 2048], BF16) for i in range(2)]
        return g

    def author_s2_units(g):
        passA = []
        for (wnat, ntile, tcols, scT, ivT, wid) in (
                (W1, 64, 2048, sc1T_s, iv1T_s, "w1"),
                (W2, 16, 8192, sc2T_s, iv2T_s, "w2")):
            nhalf = tcols // 2048 if tcols > 2048 else 1
            hw = min(tcols, 2048)
            nblk = hw // 32
            for b2 in range(nhalf):
                for i in range(ntile):
                    passA.append((wnat, i, b2, hw, nblk, scT, ivT, wid))

        uidx = [0]

        def unit_prologue():
            def mz(e):
                for b in (g["scb2"][0], g["scb2"][1], g["ivb2"][0], g["ivb2"][1]):
                    e.memset(b[:, :], 0.0)
                return e.drain()
            sch.emit("vector", mz, waits=[], inc="eV")
        s2_units.append(unit_prologue)

        def mk_passA(args):
            wnat, i, b2, hw, nblk, scT, ivT, wid = args
            def unit():
                u = uidx[0]; uidx[0] += 1
                par = u % 2
                wb = g["wnat"][par]
                w_ld = [sch.cnt.get(("_s2a_va", par), None)]
                w_ld = [w for w in w_ld if w]
                ldq = "sync" if par == 0 else "scalar"
                ldslot = "dW0" if par == 0 else "dW3"
                sch.emit(ldq, (lambda wb=wb: lambda e: e.dma_start(
                    out=wb[:, 0:hw],
                    in_=wnat[i * 128:(i + 1) * 128, b2 * 2048:b2 * 2048 + hw]))(),
                    waits=w_ld, inc=ldslot, inc_amt=16)
                LD = sch.ms(ldslot)
                am2, sb2, ib2 = g["am2"][par], g["sb2"][par], g["ib2"][par]
                scb2, ivb2 = g["scb2"][par], g["ivb2"][par]

                def va(wb=wb):
                    def f(e):
                        e.reduce_max(out=am2[:, 0:nblk],
                                     in_=wb[:, 0:hw].rearrange("p (nb b) -> p nb b", b=32),
                                     axis=AX, apply_absolute_value=True)
                        e.drain()
                        e.tensor_scalar(out=sb2[:, 0:nblk], in0=am2[:, 0:nblk].bitcast(I32),
                                        scalar1=EXPMASK, scalar2=None, op0=Alu.bitwise_and)
                        e.drain()
                        e.tensor_scalar(out=sb2[:, 0:nblk], in0=sb2[:, 0:nblk],
                                        scalar1=EMIN, scalar2=ESUB,
                                        op0=Alu.max, op1=Alu.subtract)
                        e.drain()
                        e.tensor_scalar(out=ib2[:, 0:nblk], in0=sb2[:, 0:nblk],
                                        scalar1=EINV, scalar2=-1.0,
                                        op0=Alu.subtract, op1=Alu.mult)
                        e.drain()
                        e.tensor_copy(out=scb2[:, 0:nblk], in_=sb2[:, 0:nblk].bitcast(F32))
                        e.drain()
                        return e.tensor_copy(out=ivb2[:, 0:nblk],
                                             in_=ib2[:, 0:nblk].bitcast(F32))
                    return f
                w_va = [LD]
                st_prev = sch.cnt.get(("_s2a_st", par), None)
                if st_prev:
                    w_va.append(st_prev)
                VA = sch.emit("vector", va(), waits=w_va, inc="eV")
                sch.cnt[("_s2a_va", par)] = VA

                rowoff = b2 * nblk
                stq = "sync"
                stslot = "dT0" if par == 0 else "dT1"
                tra, trb = g["tra"][par], g["trb"][par]

                def mk_trs(scb2=scb2, ivb2=ivb2, tra=tra, trb=trb,
                           nblk=nblk, stslot=stslot):
                    def f(e):
                        e.dma_start_transpose(
                            out=tra[:, :],
                            in_=scb2[:, :]).then_inc(sems[stslot], 16)
                        e.dma_start_transpose(
                            out=trb[:, :],
                            in_=ivb2[:, :]).then_inc(sems[stslot], 16)
                        return None
                    return f
                w_tr = [VA]
                st_pr2 = sch.cnt.get(("_s2a_gg", par), None)
                if st_pr2:
                    w_tr.append(st_pr2)
                sch.emit(stq, mk_trs(), waits=w_tr, inc=stslot, inc_amt=32)
                TRS = sch.ms(stslot)
                sch.cnt[("_s2a_st", par)] = TRS
                gslot = "dS2" if par == 0 else "dS2b"

                def mk_sts(tra=tra, trb=trb, scT=scT, ivT=ivT, i=i,
                           rowoff=rowoff, nblk=nblk, gslot=gslot):
                    def f(e):
                        dq(e, gslot,
                           scT[rowoff:rowoff + nblk, i * 128:(i + 1) * 128],
                           tra[0:nblk, 0:128])
                        dq(e, gslot,
                           ivT[rowoff:rowoff + nblk, i * 128:(i + 1) * 128],
                           trb[0:nblk, 0:128])
                        return None
                    return f
                sch.emit("gpsimd", mk_sts(), waits=[TRS], inc=gslot, inc_amt=32)
                sch.cnt[("_s2a_gg", par)] = sch.ms(gslot)
                sch.cnt[("_s2a_done", u)] = sch.ms(gslot)
                prev_other = sch.cnt.get(("_s2a_gg", 1 - par), None)
                if prev_other is not None:
                    sch.cnt[("_s2a_doneo", u)] = prev_other
            return unit

        for args in passA:
            s2_units.append(mk_passA(args))

        passB = []
        for (wt, ntile, width, nq, scT, ivT, wqT, wid) in (
                (W1T, 16, DF, 4, sc1T_s, iv1T_s, W1R_s, "w1"),
                (W2T, 64, D, 1, sc2T_s, ivT2 if False else iv2T_s, W2R_s, "w2")):
            for qo in range(nq):
                for i in range(ntile):
                    passB.append((wt, i, qo, width, scT, ivT, wqT, wid))

        ubidx = [0]

        def mk_passB(args):
            wt, i, qo, width, scT, ivT, wqT, wid = args
            def unit():
                u = ubidx[0]; ubidx[0] += 1
                par = u % 2
                wtb, wqb = g["wt"][par], g["wq"][par]
                w_ld = [sch.cnt.get(("_s2b_vq", par), None)]
                w_ld = [w for w in w_ld if w]
                if wid == "w1":
                    gA_hi = qo * 16 + 15
                else:
                    gA_hi = 64 + (i // 16) * 16 + 15
                for k in (gA_hi, gA_hi - 1):
                    gd = sch.cnt.get(("_s2a_done", k), None)
                    if gd:
                        w_ld.append(gd)
                    gdo = sch.cnt.get(("_s2a_doneo", k), None)
                    if gdo:
                        w_ld.append(gdo)
                ldq = "sync" if par == 0 else "scalar"
                ldslot = "dW1" if par == 0 else "dW0"
                sch.emit(ldq, (lambda wtb=wtb: lambda e: e.dma_start(
                    out=wtb[:, :],
                    in_=wt[i * 128:(i + 1) * 128,
                           qo * 2048:(qo + 1) * 2048]))(),
                    waits=w_ld, inc=ldslot, inc_amt=16)
                LD = sch.ms(ldslot)

                expslot = "dW2" if par == 0 else "dW3"
                ivb, scb = g["ive"][par], g["sce"][par]

                def mk_exp(ivb=ivb, scb=scb, expslot=expslot):
                    def f(e):
                        for gg in range(4):
                            dq(e, expslot, ivb[32 * gg:32 * gg + 32, :],
                               ivT[4 * i + gg:4 * i + gg + 1,
                                   qo * 2048:(qo + 1) * 2048].broadcast_to([32, 2048]))
                            dq(e, expslot, scb[32 * gg:32 * gg + 32, :],
                               scT[4 * i + gg:4 * i + gg + 1,
                                   qo * 2048:(qo + 1) * 2048].broadcast_to([32, 2048]))
                        return None
                    return f
                w_exp = [sch.cnt.get(("_s2b_vq", par), None)]
                w_exp = [w for w in w_exp if w]
                for k in (gA_hi, gA_hi - 1):
                    gd = sch.cnt.get(("_s2a_done", k), None)
                    if gd:
                        w_exp.append(gd)
                    gdo = sch.cnt.get(("_s2a_doneo", k), None)
                    if gdo:
                        w_exp.append(gdo)
                expq = "scalar" if par == 0 else "sync"
                sch.emit(expq, mk_exp(), waits=w_exp, inc=expslot, inc_amt=16 * 8)
                EXP = sch.ms(expslot)

                def vq(wtb=wtb, wqb=wqb, ivb=ivb, scb=scb):
                    def f(e):
                        e.tensor_tensor(out=g["uq"][:, :], in0=wtb[:, :],
                                        in1=ivb[:, :], op=Alu.mult)
                        e.drain()
                        e.tensor_scalar(out=g["rq"][:, :], in0=g["uq"][:, :],
                                        scalar1=MAGIC, scalar2=MAGIC,
                                        op0=Alu.add, op1=Alu.subtract)
                        e.drain()
                        return e.scalar_tensor_tensor(
                            out=wqb[:, :], in0=g["rq"][:, :], scalar=127.0,
                            in1=scb[:, :], op0=Alu.min, op1=Alu.mult)
                    return f
                w_vq = [LD, EXP]
                g_prev = sch.cnt.get(("_s2b_g", par), None)
                if g_prev:
                    w_vq.append(g_prev)
                VQ = sch.emit("vector", vq(), waits=w_vq, inc="eV")
                sch.cnt[("_s2b_vq", par)] = VQ

                wqslot = "dS2q" if par == 0 else "dS2qb"
                if wid == "w1":
                    st_ap = bass.AP(wqT.tensor, qo * 32768 + i * 512,
                                    [[NFC * DF, 128], [8192, 4], [1, 512]])
                else:
                    st_ap = bass.AP(wqT.tensor, i * 128,
                                    [[64 * D, 128], [8192, 16], [1, 128]])
                sch.emit("gpsimd", (lambda wqb=wqb, st_ap=st_ap: lambda e:
                    e.dma_start(out=st_ap, in_=wqb[:, :]))(),
                    waits=[VQ], inc=wqslot, inc_amt=16)
                sch.cnt[("_s2b_g", par)] = sch.ms(wqslot)
            return unit

        for args in passB:
            s2_units.append(mk_passB(args))

    BAR = sch.barrier()

    s3keep = contextlib.ExitStack()
    K_t = SB(s3keep, "s3_K", [128, NFC * S], BF16)
    qlatT_t = SB(s3keep, "s3_qlatT", [128, 4 * S], BF16)
    s3 = contextlib.ExitStack()
    xbT_full = SB(s3, "s3_xbTf", [128, NFC * 2048], BF16)
    ckvT_t = SB(s3, "s3_ckvT", [128, 4 * S], BF16)
    wbuf = SB(s3, "s3_wbuf", [128, 16384], BF16)
    bias_kv_t = SB(s3, "s3_bkv", [128, 4], F32)
    bias_q_t = SB(s3, "s3_bq", [128, 4], F32)
    bias_ku_t = SB(s3, "s3_bku", [128, 8], F32)
    bias_kr_t = SB(s3, "s3_bkr", [128, 8], F32)
    bvu_rep = SB(s3, "s3_bvu", [128, D], F32)
    v_t = SB(s3, "s3_v", [128, D], BF16)
    p_a = [PS(s3, f"s3_pa{i}", [128, 2048], F32) for i in range(2)]

    def s3_bias(e):
        bias_cols(e, bias_kv_t, b_kv, 4).then_inc(sems["d3"], 16)
        bias_cols(e, bias_q_t, b_q, 4).then_inc(sems["d3"], 16)
        bias_cols(e, bias_ku_t, b_ku, 8).then_inc(sems["d3"], 16)
        bias_cols(e, bias_kr_t, b_kr, 8).then_inc(sems["d3"], 16)
        dq(e, "d3", bvu_rep[:, :], b_vu.broadcast_to([128, D]))
        return None
    sch.emit("sync", s3_bias, waits=BAR, inc="d3", inc_amt=16 * 5)
    S3BIAS = sch.ms("d3")

    def ld_xbT(e):
        for c in range(NFC):
            dq(e, "d4", xbT_full[:, c * 2048:(c + 1) * 2048],
               xbT_s[c * 128:(c + 1) * 128, :])
        return None
    sch.emit("sync", ld_xbT, waits=BAR, inc="d4", inc_amt=16 * NFC)
    XBT_LD = sch.ms("d4")

    def proj(dst, wsrc, wcols, bias_t, nout, name):
        sch.emit("gpsimd", (lambda: lambda e:
                 e.dma_start(out=wbuf[:, 0:NFC * wcols],
                             in_=w_chunked(wsrc, wcols, NFC)))(),
                 waits=[XBT_LD, sch.ms("eP")], inc="gA", inc_amt=16)
        WLD = sch.ms("gA")
        for oc in range(nout):
            ps = p_a[oc % 2]
            w_ps = [sch.cnt.get(("pa_ev", oc % 2), None)]
            w_ps = [w for w in w_ps if w]

            def mk_mm(oc=oc, wcols=wcols, ps=ps):
                def f(e):
                    last = None
                    for c in range(NFC):
                        for j in range(4):
                            last = e.matmul(
                                out=ps[:, j * 512:(j + 1) * 512],
                                lhsT=wbuf[:, c * wcols + oc * 128:
                                          c * wcols + (oc + 1) * 128],
                                rhs=xbT_full[:, c * 2048 + j * 512:
                                             c * 2048 + (j + 1) * 512],
                                start=(c == 0), stop=(c == NFC - 1))
                    return last
                return f
            sch.emit("tensor", mk_mm(), waits=[WLD] + w_ps, inc="eP")
            MM = sch.ms("eP")

            def mk_ev(dst=dst, oc=oc, bias_t=bias_t, ps=ps):
                return lambda e: e.activation(out=dst[:, oc * 2048:(oc + 1) * 2048],
                                              in_=ps[:, :], func=Act.Identity,
                                              bias=bias_t[:, oc:oc + 1], scale=1.0)
            E = sch.emit("scalar", mk_ev(), waits=[MM, S3BIAS], inc="eS")
            sch.cnt[(name, oc)] = E
            sch.cnt[("pa_ev", oc % 2)] = E

    proj(ckvT_t, w_kv, DL, bias_kv_t, 4, "ckv")
    proj(qlatT_t, w_q, DL, bias_q_t, 4, "qlat")
    sch.emit("gpsimd", (lambda: lambda e:
             e.dma_start(out=wbuf[:, 0:NFC * 1024],
                         in_=w_chunked(w_kr, 1024, NFC)))(),
             waits=[XBT_LD, sch.ms("eP")], inc="gA", inc_amt=16)
    WLD = sch.ms("gA")
    for oc in range(8):
        ps = p_a[oc % 2]
        w_ps = [sch.cnt.get(("pa_ev", oc % 2), None)]
        w_ps = [w for w in w_ps if w]

        def mk_mm(oc=oc, ps=ps):
            def f(e):
                last = None
                for c in range(NFC):
                    for j in range(4):
                        last = e.matmul(
                            out=ps[:, j * 512:(j + 1) * 512],
                            lhsT=wbuf[:, c * 1024 + oc * 128:c * 1024 + (oc + 1) * 128],
                            rhs=xbT_full[:, c * 2048 + j * 512:c * 2048 + (j + 1) * 512],
                            start=(c == 0), stop=(c == NFC - 1))
                return last
            return f
        sch.emit("tensor", mk_mm(), waits=[WLD] + w_ps, inc="eP")
        MM = sch.ms("eP")

        def mk_ev(oc=oc, ps=ps):
            return lambda e: e.activation(
                out=K_t[:, (8 + oc) * 2048:(9 + oc) * 2048], in_=ps[:, :],
                func=Act.Identity, bias=bias_kr_t[:, oc:oc + 1], scale=1.0)
        E = sch.emit("scalar", mk_ev(), waits=[MM, S3BIAS], inc="eS")
        sch.cnt[("kr", oc)] = E
        sch.cnt[("pa_ev", oc % 2)] = E
    KR_DONE = sch.barrier()

    sch.emit("gpsimd", (lambda: lambda e:
             e.dma_start(out=wbuf[:, 0:4 * 1024],
                         in_=w_chunked(w_ku, 1024, 4)))(),
             waits=KR_DONE, inc="gA", inc_amt=16)
    WLD = sch.ms("gA")
    for oc in range(8):
        ps = p_a[oc % 2]
        w_ps = [sch.cnt.get(("pa_ev", oc % 2), None)]
        w_ps = [w for w in w_ps if w]

        def mk_mm(oc=oc, ps=ps):
            def f(e):
                last = None
                for c in range(4):
                    for j in range(4):
                        last = e.matmul(
                            out=ps[:, j * 512:(j + 1) * 512],
                            lhsT=wbuf[:, c * 1024 + oc * 128:c * 1024 + (oc + 1) * 128],
                            rhs=ckvT_t[:, c * 2048 + j * 512:c * 2048 + (j + 1) * 512],
                            start=(c == 0), stop=(c == 3))
                return last
            return f
        sch.emit("tensor", mk_mm(), waits=[WLD] + w_ps, inc="eP")
        MM = sch.ms("eP")

        def mk_ev(oc=oc, ps=ps):
            return lambda e: e.activation(
                out=K_t[:, oc * 2048:(oc + 1) * 2048], in_=ps[:, :],
                func=Act.Identity, bias=bias_ku_t[:, oc:oc + 1], scale=1.0)
        E = sch.emit("scalar", mk_ev(), waits=[MM, S3BIAS], inc="eS")
        sch.cnt[("ku", oc)] = E
        sch.cnt[("pa_ev", oc % 2)] = E

    sch.emit("gpsimd", (lambda: lambda e:
             e.dma_start(out=wbuf[:, 4096:4096 + 4 * 2048],
                         in_=w_chunked(w_vu, 2048, 4)))(),
             waits=KR_DONE, inc="gB", inc_amt=16)
    WVLD = sch.ms("gB")
    for tt_ in range(NT):
        ps = p_a[tt_ % 2]
        w_ps = [sch.cnt.get(("pa_ev", tt_ % 2), None)]
        w_ps = [w for w in w_ps if w]

        def mk_mm(tt_=tt_, ps=ps):
            def f(e):
                last = None
                for c in range(4):
                    for j in range(4):
                        last = e.matmul(
                            out=ps[:, j * 512:(j + 1) * 512],
                            lhsT=ckvT_t[:, c * 2048 + tt_ * 128:c * 2048 + (tt_ + 1) * 128],
                            rhs=wbuf[:, 4096 + c * 2048 + j * 512:
                                     4096 + c * 2048 + (j + 1) * 512],
                            start=(c == 0), stop=(c == 3))
                return last
            return f
        sch.emit("tensor", mk_mm(), waits=[WVLD] + w_ps, inc="eP")
        MM = sch.ms("eP")

        def mk_ev(tt_=tt_, ps=ps):
            def f(e):
                return e.tensor_tensor(out=v_t[:, :], in0=ps[:, :],
                                       in1=bvu_rep[:, :], op=Alu.add)
            return f
        w_evv = [MM, S3BIAS]
        vst_prev = sch.cnt.get(("v", tt_ - 1), None)
        if vst_prev:
            w_evv.append(vst_prev)
        EVv = sch.emit("vector", mk_ev(), waits=w_evv, inc="eV")
        sch.cnt[("pa_ev", tt_ % 2)] = EVv
        EV = sch.ms("eV")
        G = sch.emit("gpsimd", (lambda tt_=tt_: lambda e:
                     e.dma_start(out=V_s[tt_ * 128:(tt_ + 1) * 128, :], in_=v_t[:, :]))(),
                     waits=[EV], inc="dS3", inc_amt=16)
        sch.cnt[("v", tt_)] = G
    s3.close()
    BAR = sch.barrier()

    s4 = contextlib.ExitStack()
    wqu_t = SB(s4, "s4_wqu", [128, 4 * 1024], BF16)
    wqr_t = SB(s4, "s4_wqr", [128, 4 * 1024], BF16)
    bias_qu_t = SB(s4, "s4_bqu", [128, 8], F32)
    bias_qr_t = SB(s4, "s4_bqr", [128, 8], F32)
    qt_h2 = [SB(s4, f"s4_qt{i}", [128, S], BF16) for i in range(2)]
    vh_t2 = [SB(s4, f"s4_vh{i}", [128, NT * 128], BF16) for i in range(2)]
    pt_b = [SB(s4, f"s4_pt{i}", [128, 512], BF16) for i in range(2)]
    nrow = SB(s4, "s4_nrow", [1, 256], F32)
    negb2 = [SB(s4, f"s4_negb{i}", [128, 1], F32) for i in range(2)]
    rz_b = SB(s4, "s4_rz", [128, 16], F32)
    zrow_b = [SB(s4, f"s4_zrow{i}", [1, 512], F32) for i in range(2)]
    ac_hb = [SB(s4, f"s4_achb{i}", [128, 512], BF16) for i in range(2)]
    ac_ev = [SB(s4, f"s4_acev{i}", [128, 512], BF16) for i in range(2)]
    p_q = PS(s4, "s4_pq", [128, 1024], F32)
    p_sc = [PS(s4, f"s4_psc{i}", [128, 512], F32) for i in range(2)]
    p_pvT = [PS(s4, f"s4_ppv{i}", [128, 512], F32) for i in range(2)]
    p_z = PS(s4, "s4_pz", [1, 512], F32)
    p_tb = PS(s4, "s4_ptb", [128, 1024], BF16)
    maskslab_t = SB(s4, "s4_maskslab", [128, 2048], BF16)
    zrow0_t = SB(s4, "s4_zeror", [1, 512], BF16)
    g2 = author_s2_alloc(s4)
    g2["ptr2"] = p_tb
    sch.emit("sync", lambda e: (
        dq(e, "dC", maskslab_t[:, :], maskslab[:, :]),)[-1] and None,
        waits=BAR, inc="dC", inc_amt=16)
    MSK_LD = sch.ms("dC")
    sch.emit("vector", lambda e: (e.memset(zrow0_t[:, :], 0.0), e.drain())[-1],
             waits=BAR, inc="eV")
    Z0_RDY = sch.ms("eV")
    author_s2_units(g2)
    s2_drained = [0]

    def drain_s2(n):
        k = 0
        while s2_units and k < n:
            s2_units.pop(0)()
            k += 1
            s2_drained[0] += 1

    sch.emit("gpsimd", (lambda: lambda e: (
        dq(e, "gC", wqu_t[:, :], w_chunked(w_qu, 1024, 4)),
        dq(e, "gC", wqr_t[:, :], w_chunked(w_qr, 1024, 4)))[-1] and None)(),
        waits=BAR, inc="gC", inc_amt=32)
    def s4_bias(e):
        bias_cols(e, bias_qu_t, b_qu, 8).then_inc(sems["d2"], 16)
        bias_cols(e, bias_qr_t, b_qr, 8).then_inc(sems["d2"], 16)
        return None
    sch.emit("sync", s4_bias, waits=BAR, inc="d2", inc_amt=32)
    QW_LD = [sch.ms("gC"), sch.ms("d2")]

    sqn_b = SB(s4, "s4_sqn", [128, S], BF16)
    mK2row = SB(s4, "s4_mK2row", [1, NH], F32)
    ztp_b = SB(s4, "s4_ztp", [128, 16], F32)
    KN_DONE = None
    for h in range(NH):
        w_sq = [BAR[0]] if h == 0 else [sch.cnt[("kn_red", h - 1)]]
        sch.emit("vector", (lambda h=h: lambda e:
                 e.tensor_tensor(out=sqn_b[:, :],
                                 in0=K_t[:, h * 2048:(h + 1) * 2048],
                                 in1=K_t[:, h * 2048:(h + 1) * 2048],
                                 op=Alu.mult))(),
                 waits=BAR + w_sq, inc="eV")
        KSQ = sch.ms("eV")
        KEV = None
        for j in range(4):
            bk = (j % 2) * 512
            w_nm = [KSQ]
            pv_ev = sch.cnt.get(("kn_evj", h, j - 2), None) or \
                    (sch.cnt.get(("kn_evj", h - 1, j + 2), None) if h > 0 else None)
            if pv_ev:
                w_nm.append(pv_ev)
            sch.emit("tensor", (lambda j=j, bk=bk: lambda e: e.matmul(
                out=p_q[0:1, bk:bk + 512], lhsT=ones_t[:, 0:1],
                rhs=sqn_b[:, j * 512:(j + 1) * 512], start=True, stop=True))(),
                waits=w_nm, inc="eP")
            KNM = sch.ms("eP")
            KEV = sch.emit("vector", (lambda j=j, bk=bk: lambda e:
                           e.reduce_max(out=nrow[0:1, 132 + j:133 + j],
                                        in_=p_q[0:1, bk:bk + 512], axis=AX))(),
                           waits=[KNM], inc="eV")
            sch.cnt[("kn_evj", h, j)] = KEV
        sch.cnt[("kn_ev", h)] = KEV
        KRD = sch.emit("vector", (lambda h=h: lambda e: (
            e.drain(),
            e.reduce_max(out=mK2row[0:1, h:h + 1], in_=nrow[0:1, 132:136],
                         axis=AX))[-1])(), waits=[KEV], inc="eV")
        sch.cnt[("kn_red", h)] = KRD
        KN_DONE = KRD

    JW = (2, 3, 5, 6)

    def emit_fin(h_, j_, negb_):
        def mk_ztp(j_=j_):
            def f(e):
                last = None
                for t4 in range(4):
                    last = e.transpose(
                        out=p_q[:, 17 + j_ * 4 + t4:18 + j_ * 4 + t4],
                        in_=zrow_b[j_ % 2][0:1, t4 * 128:(t4 + 1) * 128],
                        identity=identf_t[0:1, 0:1])
                return last
            return f
        w_ztp = [sch.cnt[("s4zev", h_, j_)]]
        sch.emit("tensor", mk_ztp(), waits=w_ztp, inc="eP")
        ZTP = sch.ms("eP")
        sch.cnt[("s4ztp", h_, j_)] = ZTP
        w_ze = [ZTP]
        if j_ == 0 and h_ > 0:
            w_ze.append(sch.cnt[("s4zrc", h_ - 1, 3)])
        ZTPE = sch.emit("scalar", (lambda j_=j_: lambda e:
                        e.copy(out=ztp_b[:, j_ * 4:(j_ + 1) * 4],
                               in_=p_q[:, 17 + j_ * 4:17 + (j_ + 1) * 4]))(),
                        waits=w_ze, inc="eS")
        sch.cnt[("s4ztpe", h_)] = ZTPE
        w_rc = [ZTPE]
        if j_ == 0 and h_ > 0:
            w_rc.append(sch.cnt[("s4pve", h_ - 1, 3)])
        sch.emit("vector", (lambda j_=j_: lambda e:
                 e.reciprocal(out=rz_b[:, j_ * 4:(j_ + 1) * 4],
                              in_=ztp_b[:, j_ * 4:(j_ + 1) * 4]))(),
                 waits=w_rc, inc="eV")
        ZRC = sch.ms("eV")
        sch.cnt[("s4zrc", h_, j_)] = ZRC
        w_scl = [ZRC]
        tb_pr = sch.cnt.get(("s4tb", h_, j_ - 2), None) or \
                (sch.cnt.get(("s4tb", h_ - 1, j_ + 2), None) if h_ > 0 else None)
        if tb_pr:
            w_scl.append(tb_pr)

        def mk_scl(j_=j_):
            def f(e):
                last = None
                for t4 in range(4):
                    last = e.activation(
                        out=ac_hb[j_ % 2][:, t4 * 128:(t4 + 1) * 128],
                        in_=p_pvT[j_ % 2][:, t4 * 128:(t4 + 1) * 128],
                        func=Act.Identity,
                        scale=rz_b[:, j_ * 4 + t4:j_ * 4 + t4 + 1])
                return last
            return f
        PVE = sch.emit("scalar", mk_scl(), waits=w_scl, inc="eS")
        sch.cnt[("s4pve", h_, j_)] = PVE

    def emit_tb(h_, j_):
        w_tb = [sch.cnt[("s4pve", h_, j_)]]
        tbe_pr = sch.cnt.get(("s4tbe", h_, j_ - 1), None) or \
                 (sch.cnt.get(("s4tbe", h_ - 1, 3), None)
                  if h_ > 0 and j_ == 0 else None)
        if tbe_pr:
            w_tb.append(tbe_pr)

        def mk_tb(h_=h_, j_=j_):
            def f(e):
                last = None
                for t4 in range(4):
                    last = e.transpose(
                        out=p_tb[:, 512 + t4 * 128:512 + (t4 + 1) * 128],
                        in_=ac_hb[j_ % 2][:, t4 * 128:(t4 + 1) * 128],
                        identity=ident_t[:, :])
                return last
            return f
        sch.emit("tensor", mk_tb(), waits=w_tb, inc="eP")
        TB = sch.ms("eP")
        sch.cnt[("s4tb", h_, j_)] = TB
        w_te = [TB]
        st_pr = sch.cnt.get(("s4st", h_, j_ - 2), None) or \
                (sch.cnt.get(("s4st", h_ - 1, j_ + 2), None) if h_ > 0 else None)
        if st_pr:
            w_te.append(st_pr)
        TBE = sch.emit("scalar", (lambda h_=h_, j_=j_: lambda e:
                       e.copy(out=ac_ev[j_ % 2][:, :],
                              in_=p_tb[:, 512:1024]))(),
                       waits=w_te, inc="eS")
        sch.cnt[("s4tbe", h_, j_)] = TBE
        slot4 = "dS4" if j_ % 2 == 0 else "dS4b"
        sch.emit("gpsimd", (lambda h_=h_, j_=j_: lambda e:
                 e.dma_start(out=acT_s[h_ * 128:(h_ + 1) * 128,
                                       j_ * 512:(j_ + 1) * 512],
                             in_=ac_ev[j_ % 2][:, :]))(),
                 waits=[TBE], inc=slot4, inc_amt=16)
        sch.cnt[("s4st", h_, j_)] = sch.ms(slot4)

    for h in range(NH):
        par = h % 2
        wsrc = wqu_t if h < 8 else wqr_t
        hb = h if h < 8 else h - 8
        bsrc = bias_qu_t if h < 8 else bias_qr_t
        qt = qt_h2[par]
        vh = vh_t2[par]

        w_qgen = list(QW_LD)
        if h == 0:
            w_qgen.append(KN_DONE)
        else:
            w_qgen.append(sch.cnt[("s4ztpe", h - 1)])
        sd = sch.cnt.get(("s4scdone", h - 2), None)
        if sd:
            w_qgen.append(sd)
        QEV = None
        for qh2 in range(2):
            def mk_qmm(wsrc=wsrc, hb=hb, qh2=qh2):
                def f(e):
                    last = None
                    for c in range(4):
                        for j2 in range(2):
                            last = e.matmul(
                                out=p_q[:, j2 * 512:(j2 + 1) * 512],
                                lhsT=wsrc[:, c * 1024 + hb * 128:c * 1024 + (hb + 1) * 128],
                                rhs=qlatT_t[:, c * 2048 + qh2 * 1024 + j2 * 512:
                                            c * 2048 + qh2 * 1024 + (j2 + 1) * 512],
                                start=(c == 0), stop=(c == 3))
                    return last
                return f
            w_qm = w_qgen if qh2 == 0 else [sch.ms("eS")]
            sch.emit("tensor", mk_qmm(), waits=w_qm, inc="eP")
            QMM = sch.ms("eP")
            sch.emit("scalar", (lambda bsrc=bsrc, hb=hb, qh2=qh2, qt=qt: lambda e:
                     e.activation(out=qt[:, qh2 * 1024:(qh2 + 1) * 1024],
                                  in_=p_q[:, :], func=Act.Identity,
                                  bias=bsrc[:, hb:hb + 1], scale=1.0))(),
                     waits=[QMM], inc="eS")
            QEV = sch.ms("eS")

        if h > 0:
            emit_tb(h - 1, 3)

        w_sq4 = [QEV]
        qn_prev = sch.cnt.get(("s4qnmm", h - 1), None)
        if qn_prev:
            w_sq4.append(qn_prev)
        sch.emit("vector", (lambda qt=qt: lambda e:
                 e.tensor_tensor(out=sqn_b[:, :], in0=qt[:, :], in1=qt[:, :],
                                 op=Alu.mult))(),
                 waits=w_sq4, inc="eV")
        QSQ = sch.ms("eV")

        QN3 = None
        for j in range(4):
            bk = (j % 2) * 512
            w_nm = [QSQ]
            pv_ev = sch.cnt.get(("qn_evj", h, j - 2), None)
            if pv_ev:
                w_nm.append(pv_ev)
            sch.emit("tensor", (lambda j=j, bk=bk: lambda e: e.matmul(
                out=p_q[0:1, bk:bk + 512], lhsT=ones_t[:, 0:1],
                rhs=sqn_b[:, j * 512:(j + 1) * 512], start=True, stop=True))(),
                waits=w_nm, inc="eP")
            QNM = sch.ms("eP")
            w_cp = [QNM]
            if j == 0:
                kn_pr = sch.cnt.get(("s4qn4", h - 1), None) or KN_DONE
                w_cp.append(kn_pr)
            QN3 = sch.emit("vector", (lambda j=j, bk=bk: lambda e:
                           e.reduce_max(out=nrow[0:1, 136 + j:137 + j],
                                        in_=p_q[0:1, bk:bk + 512], axis=AX))(),
                           waits=w_cp, inc="eV")
            sch.cnt[("qn_evj", h, j)] = QN3
        sch.cnt[("s4qnmm", h)] = QNM

        w_qn4 = [QN3, KN_DONE]
        nb_prev = sch.cnt.get(("s4nb1", h - 1), None)
        if nb_prev:
            w_qn4.append(nb_prev)

        def mk_qn4(h=h):
            def f(e):
                e.drain()
                e.reduce_max(out=nrow[0:1, 128:129], in_=nrow[0:1, 136:140],
                             axis=AX)
                e.drain()
                e.tensor_tensor(out=nrow[0:1, 129:130],
                                in0=nrow[0:1, 128:129],
                                in1=mK2row[0:1, h:h + 1], op=Alu.mult)
                e.drain()
                e.tensor_scalar(out=nrow[0:1, 131:132].bitcast(I32),
                                in0=nrow[0:1, 129:130].bitcast(I32),
                                scalar1=EXPMASK, scalar2=None,
                                op0=Alu.bitwise_and)
                e.drain()
                e.tensor_scalar(out=nrow[0:1, 131:132].bitcast(I32),
                                in0=nrow[0:1, 131:132].bitcast(I32),
                                scalar1=0.5, scalar2=float(64 << 23),
                                op0=Alu.mult, op1=Alu.add)
                e.drain()
                return e.tensor_scalar(out=nrow[0:1, 130:131],
                                       in0=nrow[0:1, 131:132],
                                       scalar1=float(-SM_SCALE), scalar2=None,
                                       op0=Alu.mult)
            return f
        QN4 = sch.emit("vector", mk_qn4(), waits=w_qn4, inc="eV")
        sch.cnt[("s4qn4", h)] = QN4
        QN5 = QN4
        sch.emit("tensor", lambda e: e.matmul(out=p_q[:, 16:17],
                                              lhsT=onesf_t[0:1, :],
                                              rhs=nrow[0:1, 130:131],
                                              start=True, stop=True),
                 waits=[QN5], inc="eP")
        NB1 = sch.ms("eP")
        sch.cnt[("s4nb1", h)] = NB1
        negb = negb2[par]
        sch.emit("scalar", (lambda negb=negb: lambda e: (
                 e.copy(out=negb[:, 0:1], in_=p_q[:, 16:17]),
                 e.drain())[-1])(),
                 waits=[NB1], inc="eS")
        NB = sch.ms("eS")

        def mk_vld(h=h, vh=vh):
            return lambda e: e.dma_start(
                out=vh[:, :],
                in_=bass.AP(V_s.tensor, h * 128,
                            [[D, 128], [128 * D, NT], [1, 128]]))
        w_vld = list(BAR)
        if sd:
            w_vld.append(sd)
        sch.emit("sync", mk_vld(), waits=w_vld, inc="d4", inc_amt=16)
        VLD = sch.ms("d4")

        for j in range(4):
            q0 = j * 512
            kt_hi = 4 * j + 4
            pvb = p_pvT[j % 2]
            EXP_ = None
            for kt in range(kt_hi):
                t = kt - 4 * j
                psc = p_sc[kt % 2]
                ptb = pt_b[kt % 2]
                w_sc = [QEV]
                pr = sch.cnt.get(("s4sc", h, j, kt - 2), None)
                if pr:
                    w_sc.append(pr)
                if kt < 2:
                    ex_pr = sch.cnt.get(("s4expl", h, j - 1), None) or \
                            (sch.cnt.get(("s4expl", h - 1, 3), None)
                             if h > 0 and j == 0 else None)
                    if ex_pr:
                        w_sc.append(ex_pr)

                def mk_sc(h=h, kt=kt, q0=q0, t=t, psc=psc, qt=qt):
                    def f(e):
                        last = e.matmul(
                            out=psc[:, :],
                            lhsT=K_t[:, h * 2048 + kt * 128:h * 2048 + (kt + 1) * 128],
                            rhs=qt[:, q0:q0 + 512],
                            start=True, stop=(t < 0))
                        if t >= 0:
                            last = e.matmul(
                                out=psc[:, :],
                                lhsT=ident_t[:, :],
                                rhs=maskslab_t[:, t * 512:(t + 1) * 512],
                                start=False, stop=True)
                        return last
                    return f
                w_sc.append(MSK_LD)
                sch.emit("tensor", mk_sc(), waits=w_sc, inc="eP")
                SC = sch.ms("eP")

                w_exp = [SC]
                pv_prev = sch.cnt.get(("s4sc", h, j, kt - 2), None)
                if pv_prev:
                    w_exp.append(pv_prev)

                def mk_exp(psc=psc, ptb=ptb, negb=negb):
                    return lambda e: e.activation(
                        out=ptb[:, :], in_=psc[:, :],
                        func=Act.Exp, bias=negb[:, 0:1], scale=SM_SCALE)
                sch.emit("scalar", mk_exp(), waits=w_exp, inc="eS")
                EXP_ = sch.ms("eS")

                w_pv = [EXP_, VLD]
                if kt == 0:
                    pve_prev = sch.cnt.get(("s4pve", h, j - 2), None) or \
                               (sch.cnt.get(("s4pve", h - 1, j + 2), None)
                                if h > 0 else None)
                    if pve_prev:
                        w_pv.append(pve_prev)
                    zev_prev = sch.cnt.get(("s4zev", h, j - 1), None) or \
                               (sch.cnt.get(("s4zev", h - 1, 3), None)
                                if h > 0 and j == 0 else None)
                    if zev_prev:
                        w_pv.append(zev_prev)
                    w_pv.append(Z0_RDY)

                def mk_pv(kt=kt, ptb=ptb, kt_hi=kt_hi, pvb=pvb, vh=vh):
                    def f(e):
                        st_ = (kt == 0)
                        sp_ = (kt == kt_hi - 1)
                        if st_:
                            e.matmul(out=pvb[:, :], lhsT=zrow0_t[0:1, 0:128],
                                     rhs=zrow0_t[0:1, 0:512],
                                     start=True, stop=False)
                        for t4 in range(4):
                            e.matmul(out=pvb[:, t4 * 128:(t4 + 1) * 128],
                                     lhsT=ptb[:, t4 * 128:(t4 + 1) * 128],
                                     rhs=vh[:, kt * 128:(kt + 1) * 128],
                                     start=False, stop=False)
                        if sp_:
                            e.matmul(out=pvb[:, :], lhsT=zrow0_t[0:1, 0:128],
                                     rhs=zrow0_t[0:1, 0:512],
                                     start=False, stop=True)
                        return e.matmul(out=p_z[0:1, :],
                                        lhsT=ones_t[:, 0:1], rhs=ptb[:, :],
                                        start=st_, stop=sp_)
                    return f
                PV = sch.emit("tensor", mk_pv(), waits=w_pv, inc="eP")
                sch.cnt[("s4sc", h, j, kt)] = PV
            sch.cnt[("s4expl", h, j)] = EXP_

            w_zev = [sch.ms("eP")]
            zt_pr = sch.cnt.get(("s4ztp", h, j - 2), None) or \
                    (sch.cnt.get(("s4ztp", h - 1, j + 2), None) if h > 0 else None)
            if zt_pr:
                w_zev.append(zt_pr)
            ZEV = sch.emit("scalar", (lambda j=j: lambda e:
                           e.copy(out=zrow_b[j % 2][0:1, :], in_=p_z[0:1, :]))(),
                           waits=w_zev, inc="eS")
            sch.cnt[("s4zev", h, j)] = ZEV

            if j >= 1:
                emit_fin(h, j - 1, negb)
            if j >= 2:
                emit_tb(h, j - 2)
            if h == NH - 1 and j == 3:
                drain_s2(len(s2_units))
            else:
                wleft = sum(JW[jj] for jj in range(j + 1, 4)) + \
                        16 * (NH - 1 - h)
                drain_s2(min(len(s2_units),
                             (len(s2_units) * JW[j] + wleft) // (JW[j] + wleft)
                             if wleft > 0 else len(s2_units)))

        emit_fin(h, 3, negb)
        emit_tb(h, 2)
        sch.cnt[("s4scdone", h)] = sch.ms("eP")
    emit_tb(NH - 1, 3)
    drain_s2(len(s2_units))
    s4.close()
    s3keep.close()
    BAR = sch.barrier()

    s5 = contextlib.ExitStack()
    b1sl_b = [SB(s5, f"s5_b1sl{i}", [1, 512], F32) for i in range(2)]
    hq0T_q = SB(s5, "s5_hq0T", [128, NFC * TQ], BF16)
    w1q_b = [SB(s5, f"s5_w1q{i}", [128, NFC * 512], BF16) for i in range(2)]
    w2q_b = [SB(s5, f"s5_w2q{i}", [128, 64 * 128], BF16) for i in range(2)]
    hq1T_q = SB(s5, "s5_hq1T", [128, 64 * TQ], BF16)
    h1_b2 = [SB(s5, f"s5_h1{i}", [128, 4 * 512], BF16) for i in range(2)]
    g_b2 = [SB(s5, f"s5_g{i}", [128, 4 * 512], BF16) for i in range(2)]
    gs_b = SB(s5, "s5_gs", [128, 4 * 512], F32)
    gv_b = SB(s5, "s5_gv", [128, 4 * 512], F32)
    u5_b = SB(s5, "s5_u", [128, 4 * 512], BF16)
    hq1_b = [SB(s5, f"s5_hq1{i}", [128, 4 * 512], BF16) for i in range(3)]
    am5 = SB(s5, "s5_am", [128, 64], F32)
    sb5 = SB(s5, "s5_sb", [128, 64], I32)
    ib5 = SB(s5, "s5_ib", [128, 64], I32)
    scb5 = SB(s5, "s5_scb", [128, 64], BF16)
    ivb5 = SB(s5, "s5_ivb", [128, 64], BF16)
    h2_b = SB(s5, "s5_h2", [128, TQ], BF16)
    xbt_q = SB(s5, "s5_xbtq", [128, TQ], BF16)
    ffo_b = SB(s5, "s5_ffo", [128, TQ], BF16)
    p_m1 = [PS(s5, f"s5_pm1{i}", [128, 512], F32) for i in range(2)]
    p_t5 = [PS(s5, f"s5_pt5{i}", [128, 512], BF16) for i in range(2)]
    p_m2 = [PS(s5, f"s5_pm2{i}", [128, 512], F32) for i in range(2)]

    B1LD = None

    def emit_quant5(q, s_q):
        hqb = hq1_b[s_q % 3]
        g_b = g_b2[s_q % 2]

        def mk_q5(e, hqb=hqb, g_b=g_b):
            return quant_chain(e, g_b, hqb, am5, sb5, ib5, scb5, ivb5, 64, u5_b)
        w_q5 = [sch.cnt[("s5ge", q, s_q)]]
        te_prev = sch.cnt.get(("s5tr", q, s_q - 3, 3), None)
        if te_prev:
            w_q5.append(te_prev)
        sch.emit("vector", mk_q5, waits=w_q5, inc="eV")
        sch.cnt[("s5q5", q, s_q)] = sch.ms("eV")

    for q in range(NQ):
        w_prevq = [sch.cnt.get(("s5m1done", q - 1), None)]
        w_prevq = [w for w in w_prevq if w]

        def mk_ldh(q=q):
            def f(e):
                for c in range(NFC):
                    dq(e, "d4", hq0T_q[:, c * TQ:(c + 1) * TQ],
                       hq0T_s[c * 128:(c + 1) * 128, q * TQ:(q + 1) * TQ])
                return None
            return f
        sch.emit("sync", mk_ldh(), waits=(BAR if q == 0 else []) + w_prevq,
                 inc="d4", inc_amt=16 * NFC)
        HLD = sch.ms("d4")

        for s_ in range(NFC):
            w1b = w1q_b[s_ % 2]
            w_w1 = [sch.cnt.get(("s5w1", q, s_ - 2), None)
                    or (sch.cnt.get(("s5w1", q - 1, NFC - 2 + s_), None)
                        if q > 0 else None)]
            w_w1 = [w for w in w_w1 if w]

            w1slot = "d%d" % (s_ % 2)

            def mk_ldw1(s_=s_, w1b=w1b, half=0):
                return lambda e: e.dma_start(
                    out=w1b[:, half * 4096:(half + 1) * 4096],
                    in_=W1R_s[:, s_ * 8192 + half * 4096:
                              s_ * 8192 + (half + 1) * 4096])
            w_w1b = w_w1 + ((BAR + [sch.ms("dS2q"), sch.ms("dS2qb")])
                            if q == 0 and s_ < 2 else [])
            sch.emit("scalar", mk_ldw1(s_, w1b, 0),
                     waits=w_w1b, inc=w1slot, inc_amt=16)
            sch.emit("sync", mk_ldw1(s_, w1b, 1),
                     waits=w_w1b, inc=w1slot, inc_amt=16)
            W1LD = sch.ms(w1slot)
            b1b = b1sl_b[s_ % 2]
            w_b1 = [sch.cnt.get(("s5b1", q, s_ - 2), None)
                    or (sch.cnt.get(("s5b1", q - 1, NFC - 2 + s_), None)
                        if q > 0 else None)]
            w_b1 = [w for w in w_b1 if w]
            b1slot = "dL" if s_ % 2 == 0 else "dC"
            sch.emit("sync", (lambda b1b=b1b, s_=s_: lambda e: e.dma_start(
                out=b1b[:, :],
                in_=b1[0:1, s_ * 512:(s_ + 1) * 512]))(),
                waits=w_b1, inc=b1slot, inc_amt=16)
            B1SL = sch.ms(b1slot)

            M1last = None
            for tt_ in range(4):
                pm = p_m1[tt_ % 2]
                w_pm = [sch.cnt.get(("s5ev", q, s_, tt_ - 2), None)]
                if tt_ < 2 and s_ > 0:
                    w_pm.append(sch.cnt.get(("s5ev", q, s_ - 1, tt_ + 2), None))
                w_pm = [w for w in w_pm if w]

                def mk_m1(tt_=tt_, w1b=w1b, pm=pm, b1b=b1b):
                    def f(e):
                        e.matmul(out=pm[:, :], lhsT=onesf_t[0:1, :],
                                 rhs=b1b[0:1, :], start=True, stop=False)
                        last = None
                        for c in range(NFC):
                            last = e.matmul(
                                out=pm[:, :],
                                lhsT=hq0T_q[:, c * TQ + tt_ * 128:c * TQ + (tt_ + 1) * 128],
                                rhs=w1b[:, c * 512:(c + 1) * 512],
                                start=False, stop=(c == NFC - 1))
                        return last
                    return f
                sch.emit("tensor", mk_m1(), waits=[HLD, W1LD, B1SL] + w_pm,
                         inc="eP")
                M1 = sch.ms("eP")
                M1last = M1

                h1_b = h1_b2[s_ % 2]

                def mk_ev1(tt_=tt_, pm=pm, h1_b=h1_b):
                    return lambda e: e.copy(
                        out=h1_b[:, tt_ * 512:(tt_ + 1) * 512], in_=pm[:, :])
                w_e1 = [M1]
                if tt_ == 0:
                    ge_prev = sch.cnt.get(("s5ge", q, s_ - 2), None) or \
                              (sch.cnt.get(("s5ge", q - 1, NFC - 2 + s_), None)
                               if q > 0 and s_ < 2 else None)
                    if ge_prev:
                        w_e1.append(ge_prev)
                E1 = sch.emit("scalar", mk_ev1(), waits=w_e1, inc="eS")
                sch.cnt[("s5ev", q, s_, tt_)] = E1
            sch.cnt[("s5b1", q, s_)] = M1last
            h1_b = h1_b2[s_ % 2]
            w_sq5 = []
            gv_prev = sch.cnt.get(("s5gv", q, s_ - 1), None) or \
                      (sch.cnt.get(("s5gv", q - 1, NFC - 1), None)
                       if q > 0 and s_ == 0 else None)
            if gv_prev:
                w_sq5.append(gv_prev)
            sch.emit("scalar", (lambda h1_b=h1_b: lambda e: (
                e.drain(),
                e.activation(out=gs_b[:, :], in_=h1_b[:, :],
                             func=Act.Square))[-1])(),
                waits=w_sq5, inc="eS")
            GSQ = sch.ms("eS")
            def mk_gel1(h1_b=h1_b):
                def f(e):
                    e.tensor_scalar(out=gs_b[:, :], in0=gs_b[:, :],
                                    scalar1=0.044715, scalar2=1.0,
                                    op0=Alu.mult, op1=Alu.add)
                    e.drain()
                    return e.tensor_tensor(out=gv_b[:, :], in0=gs_b[:, :],
                                           in1=h1_b[:, :], op=Alu.mult)
                return f
            sch.emit("vector", mk_gel1(), waits=[GSQ], inc="eV")
            GV = sch.ms("eV")
            sch.cnt[("s5gv", q, s_)] = GV
            if s_ >= 1:
                emit_quant5(q, s_ - 1)
            sch.emit("scalar", (lambda: lambda e:
                     e.activation(out=gv_b[:, :], in_=gv_b[:, :], func=Act.Tanh,
                                  scale=float(0.7978845608028654)))(),
                     waits=[GV], inc="eS")
            GT = sch.ms("eS")
            def mk_gel2(h1_b=h1_b, g_b=g_b2[s_ % 2]):
                def f(e):
                    e.tensor_scalar(out=gv_b[:, :], in0=gv_b[:, :],
                                    scalar1=0.5, scalar2=0.5,
                                    op0=Alu.mult, op1=Alu.add)
                    e.drain()
                    return e.tensor_tensor(out=g_b[:, :], in0=gv_b[:, :],
                                           in1=h1_b[:, :], op=Alu.mult)
                return f
            sch.emit("vector", mk_gel2(), waits=[GT], inc="eV")
            GE = sch.ms("eV")
            sch.cnt[("s5ge", q, s_)] = GE

            if s_ == NFC - 1:
                emit_quant5(q, s_)
            sch.cnt[("s5w1", q, s_)] = sch.ms("eP")
            for s_d in ([s_ - 2] if s_ < NFC - 1 else [s_ - 2, s_ - 1, s_]):
                if s_d < 0:
                    continue
                hqd = hq1_b[s_d % 3]
                Q5d = sch.cnt[("s5q5", q, s_d)]
                for tg in range(4):
                    pt5 = p_t5[tg % 2]

                    def mk_tr5(tg=tg, hqd=hqd, pt5=pt5):
                        def f(e):
                            last = None
                            for k in range(4):
                                last = e.transpose(
                                    out=pt5[:, k * 128:(k + 1) * 128],
                                    in_=hqd[:, tg * 512 + k * 128:tg * 512 + (k + 1) * 128],
                                    identity=ident_t[:, :])
                            return last
                        return f
                    w_t5 = [Q5d]
                    pt_prev = sch.cnt.get(("s5tr", q, s_d, tg - 2), None)
                    if pt_prev:
                        w_t5.append(pt_prev)
                    elif tg < 2:
                        pr = sch.cnt.get(("s5tr", q, s_d - 1, tg + 2), None)
                        if pr:
                            w_t5.append(pr)
                    sch.emit("tensor", mk_tr5(), waits=w_t5, inc="eP")
                    TR5 = sch.ms("eP")

                    def mk_tev(s_d=s_d, tg=tg, pt5=pt5):
                        def f(e):
                            last = None
                            for k in range(4):
                                oc = s_d * 4 + k
                                last = e.copy(
                                    out=hq1T_q[:, oc * TQ + tg * 128:oc * TQ + (tg + 1) * 128],
                                    in_=pt5[:, k * 128:(k + 1) * 128])
                            return last
                        return f
                    TE = sch.emit("scalar", mk_tev(), waits=[TR5], inc="eS")
                    sch.cnt[("s5tr", q, s_d, tg)] = TE
        sch.cnt[("s5m1done", q)] = sch.ms("eP")
        M1DONE = sch.barrier()

        for hc in range(NFC):
            w2b = w2q_b[hc % 2]
            w_w2 = [sch.cnt.get(("s5w2", q, hc - 2), None)
                    or (sch.cnt.get(("s5w2", q - 1, NFC - 2 + hc), None)
                        if q > 0 else None)]
            w_w2 = [w for w in w_w2 if w]

            def mk_ldw2(hc=hc, w2b=w2b, lo=0, hi=8192):
                return lambda e: e.dma_start(
                    out=w2b[:, lo:hi],
                    in_=W2R_s[:, hc * 8192 + lo:hc * 8192 + hi])
            w2slot = "d%d" % (5 + hc % 2)
            w2gslot = "g5a" if hc % 2 == 0 else "g5b"
            w_w2b = w_w2 + ((BAR + [sch.ms("dS2q"), sch.ms("dS2qb")])
                            if q == 0 and hc < 2 else [])
            sch.emit("scalar", mk_ldw2(hc, w2b, 0, 2688), waits=w_w2b,
                     inc=w2slot, inc_amt=16)
            sch.emit("sync", mk_ldw2(hc, w2b, 2688, 5504), waits=w_w2b,
                     inc=w2slot, inc_amt=16)
            sch.emit("gpsimd", mk_ldw2(hc, w2b, 5504, 8192), waits=w_w2b,
                     inc=w2gslot, inc_amt=16)
            W2LD = sch.ms(w2slot)
            W2GLD = sch.ms(w2gslot)
            def mk_ldxb(hc=hc, q=q):
                return lambda e: e.dma_start(
                    out=xbt_q[:, :],
                    in_=xbT_s[hc * 128:(hc + 1) * 128, q * TQ:(q + 1) * TQ])
            w_xb = [sch.cnt.get(("s5ff", q, hc - 1), None)]
            w_xb = [w for w in w_xb if w]
            sch.emit("sync", mk_ldxb(), waits=w_xb + (M1DONE if hc == 0 else []),
                     inc="d7", inc_amt=16)
            XBLD = sch.ms("d7")

            pm2 = p_m2[hc % 2]
            w_pm2 = [sch.cnt.get(("s5h2", q, hc - 2), None)]
            w_pm2 = [w for w in w_pm2 if w]

            def mk_m2(hc=hc, w2b=w2b, pm2=pm2):
                def f(e):
                    last = None
                    for oc in range(64):
                        last = e.matmul(
                            out=pm2[:, :],
                            lhsT=w2b[:, oc * 128:(oc + 1) * 128],
                            rhs=hq1T_q[:, oc * TQ:(oc + 1) * TQ],
                            start=(oc == 0), stop=(oc == 63))
                    return last
                return f
            sch.emit("tensor", mk_m2(), waits=[W2LD, W2GLD, M1DONE[5]] + w_pm2,
                     inc="eP")
            M2 = sch.ms("eP")
            sch.cnt[("s5w2", q, hc)] = M2
            w_h2 = [M2]
            ffo_prev = sch.cnt.get(("s5ffv", q, hc - 1), None)
            if ffo_prev:
                w_h2.append(ffo_prev)
            sch.emit("scalar", (lambda hc=hc, pm2=pm2: lambda e:
                     e.activation(out=h2_b[:, :], in_=pm2[:, :], func=Act.Identity,
                                  bias=bias_b2_t[:, hc:hc + 1], scale=1.0))(),
                     waits=w_h2, inc="eS")
            sch.cnt[("s5h2", q, hc)] = sch.ms("eS")
            H2E = sch.ms("eS")

            def mk_ffo(e):
                return e.tensor_tensor(out=ffo_b[:, :], in0=h2_b[:, :],
                                       in1=xbt_q[:, :], op=Alu.add)
            w_ffo = [H2E, XBLD]
            g_prev5 = sch.cnt.get(("s5ff", q, hc - 1), None)
            if g_prev5:
                w_ffo.append(g_prev5)
            sch.emit("vector", mk_ffo, waits=w_ffo, inc="eV")
            FFO = sch.ms("eV")
            sch.cnt[("s5ffv", q, hc)] = FFO
            G = sch.emit("gpsimd", (lambda hc=hc, q=q: lambda e:
                         e.dma_start(out=ffoT_s[hc * 128:(hc + 1) * 128,
                                                q * TQ:(q + 1) * TQ],
                                     in_=ffo_b[:, :]))(),
                         waits=[FFO], inc="dS5", inc_amt=16)
            sch.cnt[("s5ff", q, hc)] = G
        sch.cnt[("s5qdone", q)] = sch.ms("dS5")
    s5.close()
    BAR = sch.barrier()

    s6 = contextlib.ExitStack()
    acT_full = SB(s6, "s6_acT", [128, NFC * S], BF16)
    wo_b = [SB(s6, f"s6_wo{i}", [128, NFC * 128], BF16) for i in range(2)]
    xt_b = [SB(s6, f"s6_xt{i}", [128, S], F32) for i in range(2)]
    ffo_t = [SB(s6, f"s6_ffo{i}", [128, S], BF16) for i in range(2)]
    e1_b = SB(s6, "s6_e1", [128, S], F32)
    out_b = [SB(s6, f"s6_out{i}", [128, S], F32) for i in range(2)]
    p_o = [PS(s6, f"s6_po{i}", [128, 2048], F32) for i in range(2)]

    def mk_ldac(e):
        for c in range(NFC):
            dq(e, "d4", acT_full[:, c * S:(c + 1) * S],
               acT_s[c * 128:(c + 1) * 128, :])
        return None
    sch.emit("sync", mk_ldac, waits=BAR, inc="d4", inc_amt=16 * NFC)
    ACLD = sch.ms("d4")

    for gc in range(NFC):
        wob = wo_b[gc % 2]
        xtb = xt_b[gc % 2]
        ffb = ffo_t[gc % 2]
        ob = out_b[gc % 2]
        w_wo = [sch.cnt.get(("s6o", gc - 2), None)]
        w_wo = [w for w in w_wo if w]

        def mk_ldwo(gc=gc, wob=wob):
            return lambda e: e.dma_start(
                out=wob[:, :],
                in_=woR[:, gc * 2048:(gc + 1) * 2048])
        woslot = "gA" if gc % 2 == 0 else "gB"
        sch.emit("gpsimd", mk_ldwo(), waits=w_wo + (BAR if gc < 2 else []),
                 inc=woslot, inc_amt=16)
        WOLD = sch.ms(woslot)

        xslot = "d%d" % (2 if gc % 2 == 0 else 7)
        def mk_ldx(gc=gc, xtb=xtb, ffb=ffb, xslot=xslot):
            def f(e):
                dq(e, xslot, xtb[:, :], xT[gc * 128:(gc + 1) * 128, :])
                dq(e, xslot, ffb[:, :], ffoT_s[gc * 128:(gc + 1) * 128, :])
                return None
            return f
        sch.emit("sync", mk_ldx(), waits=w_wo + (([ACLD] + BAR) if gc < 2 else []),
                 inc=xslot, inc_amt=32)
        XLD = sch.ms(xslot)

        po = p_o[gc % 2]
        w_po = [sch.cnt.get(("s6fin", gc - 2), None)]
        w_po = [w for w in w_po if w]

        def mk_omm(gc=gc, wob=wob, po=po):
            def f(e):
                last = None
                for c in range(NFC):
                    for j in range(4):
                        last = e.matmul(
                            out=po[:, j * 512:(j + 1) * 512],
                            lhsT=wob[:, c * 128:(c + 1) * 128],
                            rhs=acT_full[:, c * S + j * 512:c * S + (j + 1) * 512],
                            start=(c == 0), stop=(c == NFC - 1))
                return last
            return f
        OMM = sch.emit("tensor", mk_omm(), waits=[ACLD, WOLD] + w_po, inc="eP")
        sch.cnt[("s6mm", gc)] = OMM

        def mk_fin(gc=gc, po=po, xtb=xtb, ffb=ffb, ob=ob):
            def f(e):
                e.scalar_tensor_tensor(out=e1_b[:, :], in0=po[:, :],
                                       scalar=bias_o_t[:, gc:gc + 1],
                                       in1=xtb[:, :], op0=Alu.add, op1=Alu.add)
                e.drain()
                return e.tensor_tensor(out=ob[:, :], in0=e1_b[:, :],
                                       in1=ffb[:, :], op=Alu.add)
            return f
        sch.emit("vector", mk_fin(), waits=[OMM, XLD], inc="eV")
        FIN = sch.ms("eV")
        sch.cnt[("s6fin", gc)] = FIN
        oslot = "dOUTa" if gc % 2 == 0 else "dOUTb"
        G = sch.emit("gpsimd", (lambda gc=gc, ob=ob: lambda e:
                     e.dma_start(out=outT[gc * 128:(gc + 1) * 128, :], in_=ob[:, :]))(),
                     waits=[FIN], inc=oslot, inc_amt=16)
        sch.cnt[("s6o", gc)] = G
    s6.close()

    sch.emit("gpsimd", lambda e: None, waits=[sch.ms("dOUTa"), sch.ms("dOUTb")])

    with nc.Block() as block:
        @block.sync
        def _(e):
            sch.replay("sync", e, sems)

        @block.scalar
        def _(e):
            sch.replay("scalar", e, sems)

        @block.vector
        def _(e):
            sch.replay("vector", e, sems)

        @block.tensor
        def _(e):
            sch.replay("tensor", e, sems)

        @block.gpsimd
        def _(e):
            sch.replay("gpsimd", e, sems)

    ctx.close()
    return nc


_NC_CACHE = None


def kernel(**inputs):
    global _NC_CACHE
    d = inputs
    ncore = 8
    if _NC_CACHE is None:
        _NC_CACHE = build_nc()
    nc = _NC_CACHE

    ident_np = np.eye(128, dtype=BF)
    ones_np = np.ones((128, 128), dtype=BF)
    kk, qq = np.meshgrid(np.arange(128), np.arange(512), indexing="ij")
    maskslab_np = np.zeros((128, 2048), dtype=BF)
    for _t in range(4):
        maskslab_np[:, _t * 512:(_t + 1) * 512] = np.where(
            _t * 128 + kk > qq, np.float32(-1e30), np.float32(0)).astype(BF)

    W1 = np.ascontiguousarray(d["L0_ffn_W1"], dtype=np.float32)
    W2 = np.ascontiguousarray(d["L0_ffn_W2"], dtype=np.float32)
    common = dict(
        w_kv=np.ascontiguousarray(d["L0_attn_w_kv"], np.float32),
        b_kv=np.ascontiguousarray(d["L0_attn_b_kv"], np.float32).reshape(1, -1),
        w_ku=np.ascontiguousarray(d["L0_attn_w_ku"], np.float32),
        b_ku=np.ascontiguousarray(d["L0_attn_b_ku"], np.float32).reshape(1, -1),
        w_kr=np.ascontiguousarray(d["L0_attn_w_kr"], np.float32),
        b_kr=np.ascontiguousarray(d["L0_attn_b_kr"], np.float32).reshape(1, -1),
        w_vu=np.ascontiguousarray(d["L0_attn_w_vu"], np.float32),
        b_vu=np.ascontiguousarray(d["L0_attn_b_vu"], np.float32).reshape(1, -1),
        w_q=np.ascontiguousarray(d["L0_attn_w_q"], np.float32),
        b_q=np.ascontiguousarray(d["L0_attn_b_q"], np.float32).reshape(1, -1),
        w_qu=np.ascontiguousarray(d["L0_attn_w_qu"], np.float32),
        b_qu=np.ascontiguousarray(d["L0_attn_b_qu"], np.float32).reshape(1, -1),
        w_qr=np.ascontiguousarray(d["L0_attn_w_qr"], np.float32),
        b_qr=np.ascontiguousarray(d["L0_attn_b_qr"], np.float32).reshape(1, -1),
        woR=np.ascontiguousarray(
            d["L0_attn_w_o"].astype(np.float32).reshape(16, 128, 16, 128)
            .transpose(1, 2, 0, 3).reshape(128, 16 * 16 * 128)),
        b_o=np.ascontiguousarray(d["L0_attn_b_o"], np.float32).reshape(1, -1),
        ln_w=np.ascontiguousarray(d["L0_ffn_ln_w"], np.float32).reshape(1, -1),
        ln_b=np.ascontiguousarray(d["L0_ffn_ln_b"], np.float32).reshape(1, -1),
        W1=W1, W1T=np.ascontiguousarray(W1.T),
        b1=np.ascontiguousarray(d["L0_ffn_b1"], np.float32).reshape(1, -1),
        W2=W2, W2T=np.ascontiguousarray(W2.T),
        b2=np.ascontiguousarray(d["L0_ffn_b2"], np.float32).reshape(1, -1),
        ident=ident_np, ones_bf=ones_np, maskslab=maskslab_np,
        ident_f=np.eye(128, dtype=np.float32),
        ones_f=np.ones((128, 128), dtype=np.float32),
    )
    x = np.ascontiguousarray(d["L0_attn_x"], np.float32)
    in_maps = []
    for c in range(ncore):
        m = dict(common)
        m["x_nat"] = np.ascontiguousarray(x[c])
        m["xT"] = np.ascontiguousarray(x[c].T)
        in_maps.append(m)

    res = run_bass_kernel_spmd(nc, in_maps, core_ids=list(range(ncore)))
    out = np.empty((ncore, S, D), dtype=np.float32)
    for c in range(ncore):
        out[c] = res.results[c]["outT"].T
    return out


# version 22
# speedup vs baseline: 1.0157x; 1.0157x over previous
import contextlib
import numpy as np
import ml_dtypes
import concourse.bass as bass
import concourse.mybir as mybir
from concourse.bass_utils import run_bass_kernel_spmd

F32 = mybir.dt.float32
BF16 = mybir.dt.bfloat16
I32 = mybir.dt.int32
BF = ml_dtypes.bfloat16
Alu = mybir.AluOpType
Act = mybir.ActivationFunctionType
AX = mybir.AxisListType.X

S = 2048
D = 2048
DL = 512
NH = 16
HD = 128
DF = 8192
NT = S // 128
NFC = D // 128
NQ = 4
TQ = S // NQ

MAGIC = float(3 << 22)
EXPMASK = 0x7F800000
EMIN = float(7 << 23)
ESUB = float(6 << 23)
EINV = float(254 << 23)
SM_SCALE = float(1.0 / np.sqrt(128.0))

ENGINES = ("sync", "scalar", "vector", "tensor", "gpsimd")


class Sched:
    def __init__(self):
        self.ops = {e: [] for e in ENGINES}
        self.cnt = {}

    def bump(self, sem, amt):
        self.cnt[sem] = self.cnt.get(sem, 0) + amt
        return self.cnt[sem]

    def val(self, sem):
        return self.cnt.get(sem, 0)

    def ms(self, sem):
        return (sem, self.val(sem))

    def emit(self, eng, fn, waits=(), inc=None, inc_amt=1):
        self.ops[eng].append((tuple(waits), fn, inc, inc_amt))
        if inc:
            self.bump(inc, inc_amt)
        return self.ms(inc) if inc else None

    semnames = ()

    def barrier(self):
        return [self.ms(s) for s in self.semnames if self.val(s) > 0]

    def replay(self, eng_name, e, sems):
        waited = {}
        for waits, fn, inc, inc_amt in self.ops[eng_name]:
            for sem, v in waits:
                if v > 0 and waited.get(sem, -1) < v:
                    e.wait_ge(sems[sem], v)
                    waited[sem] = v
            last = fn(e)
            if inc is not None and last is not None:
                last.then_inc(sems[inc], inc_amt)


def quant_chain(e, src_bf, dst_bf, am, sb, ib, scb, ivb, nblk, u_tmp):
    n = nblk * 32
    src3 = src_bf[:, 0:n].rearrange("p (nb b) -> p nb b", b=32)
    e.reduce_max(out=am[:, 0:nblk], in_=src3, axis=AX, apply_absolute_value=True)
    e.drain()
    e.tensor_scalar(out=sb[:, 0:nblk], in0=am[:, 0:nblk].bitcast(I32),
                    scalar1=EXPMASK, scalar2=None, op0=Alu.bitwise_and)
    e.drain()
    e.tensor_scalar(out=sb[:, 0:nblk], in0=sb[:, 0:nblk], scalar1=EMIN, scalar2=ESUB,
                    op0=Alu.max, op1=Alu.subtract)
    e.drain()
    e.tensor_scalar(out=ib[:, 0:nblk], in0=sb[:, 0:nblk], scalar1=EINV, scalar2=-1.0,
                    op0=Alu.subtract, op1=Alu.mult)
    e.drain()
    e.tensor_copy(out=scb[:, 0:nblk], in_=sb[:, 0:nblk].bitcast(F32))
    e.drain()
    e.tensor_copy(out=ivb[:, 0:nblk], in_=ib[:, 0:nblk].bitcast(F32))
    e.drain()
    e.tensor_tensor(out=u_tmp[:, 0:n].rearrange("p (nb b) -> p nb b", b=32),
                    in0=src3,
                    in1=ivb[:, 0:nblk].unsqueeze(2).broadcast_to([128, nblk, 32]),
                    op=Alu.mult)
    e.drain()
    e.tensor_scalar(out=u_tmp[:, 0:n], in0=u_tmp[:, 0:n], scalar1=MAGIC, scalar2=MAGIC,
                    op0=Alu.add, op1=Alu.subtract)
    e.drain()
    return e.scalar_tensor_tensor(
        out=dst_bf[:, 0:n].rearrange("p (nb b) -> p nb b", b=32),
        in0=u_tmp[:, 0:n].rearrange("p (nb b) -> p nb b", b=32),
        scalar=127.0,
        in1=scb[:, 0:nblk].unsqueeze(2).broadcast_to([128, nblk, 32]),
        op0=Alu.min, op1=Alu.mult)


def build_nc():
    nc = bass.Bass("TRN2", target_bir_lowering=False)
    din = {}

    def DI(name, shape, dt=F32):
        din[name] = nc.dram_tensor(name, list(shape), dt, kind="ExternalInput").ap()
        return din[name]

    x_nat = DI("x_nat", [S, D])
    xT = DI("xT", [D, S])
    w_kv = DI("w_kv", [D, DL]); b_kv = DI("b_kv", [1, DL])
    w_ku = DI("w_ku", [DL, D // 2]); b_ku = DI("b_ku", [1, D // 2])
    w_kr = DI("w_kr", [D, D // 2]); b_kr = DI("b_kr", [1, D // 2])
    w_vu = DI("w_vu", [DL, D]); b_vu = DI("b_vu", [1, D])
    w_q = DI("w_q", [D, DL]); b_q = DI("b_q", [1, DL])
    w_qu = DI("w_qu", [DL, D // 2]); b_qu = DI("b_qu", [1, D // 2])
    w_qr = DI("w_qr", [DL, D // 2]); b_qr = DI("b_qr", [1, D // 2])
    woR = DI("woR", [128, NFC * D]); b_o = DI("b_o", [1, D])
    ln_w = DI("ln_w", [1, D]); ln_b = DI("ln_b", [1, D])
    W1 = DI("W1", [DF, D]); W1T = DI("W1T", [D, DF]); b1 = DI("b1", [1, DF])
    W2 = DI("W2", [D, DF]); W2T = DI("W2T", [DF, D]); b2 = DI("b2", [1, D])
    ident = DI("ident", [128, 128], BF16)
    ones_bf = DI("ones_bf", [128, 128], BF16)
    maskslab = DI("maskslab", [128, 2048], BF16)
    ident_f = DI("ident_f", [128, 128], F32)
    ones_f = DI("ones_f", [128, 128], F32)

    outT = nc.dram_tensor("outT", [D, S], F32, kind="ExternalOutput").ap()

    def DSC(name, shape, dt):
        return nc.dram_tensor(name, list(shape), dt, kind="Internal").ap()

    xbT_s = DSC("xbT_s", [D, S], BF16)
    hq0T_s = DSC("hq0T_s", [D, S], BF16)
    V_s = DSC("V_s", [S, D], BF16)
    acT_s = DSC("acT_s", [D, S], BF16)
    ffoT_s = DSC("ffoT_s", [D, S], BF16)
    sc1T_s = DSC("sc1T_s", [64, DF], BF16)
    iv1T_s = DSC("iv1T_s", [64, DF], BF16)
    sc2T_s = DSC("sc2T_s", [256, D], BF16)
    iv2T_s = DSC("iv2T_s", [256, D], BF16)
    W1R_s = DSC("W1R_s", [128, NFC * DF], BF16)
    W2R_s = DSC("W2R_s", [128, 64 * D], BF16)

    sch = Sched()
    ctx = contextlib.ExitStack()
    SEM_NAMES = ("eP", "eV", "eS", "dC", "dL",
                 "d0", "d1", "d2", "d3", "d4", "d5", "d6", "d7",
                 "gA", "gB", "gC", "dW0", "dW1", "dW2",
                 "dS1a", "dS1b", "dS2", "dS2q", "dS3", "dS4", "dS4b", "dS5",
                 "dW3", "dT0", "dT1", "dS2qb", "dS2b",
                 "dOUTa", "dOUTb")
    sems = {s_: ctx.enter_context(nc.semaphore(s_)) for s_ in SEM_NAMES}
    sch.semnames = SEM_NAMES

    def dq(e, q, out, in_):
        e.dma_start(out=out, in_=in_).then_inc(sems[q], 16)

    def w_chunked(wsrc, ncols, nchunk):
        return bass.AP(wsrc.tensor, 0,
                       [[ncols, 128], [128 * ncols, nchunk], [1, ncols]])

    def SB(st, name, shape, dt):
        return st.enter_context(nc.sbuf_tensor(name, list(shape), dt))

    def PS(st, name, shape, dt=F32):
        return st.enter_context(nc.psum_tensor(name, list(shape), dt))

    ident_t = SB(ctx, "ident_t", [128, 128], BF16)
    ones_t = SB(ctx, "ones_t", [128, 128], BF16)
    identf_t = SB(ctx, "identf_t", [128, 128], F32)
    onesf_t = SB(ctx, "onesf_t", [128, 128], F32)
    bias_o_t = SB(ctx, "bias_o_t", [128, 16], F32)
    eps_t = SB(ctx, "eps_t", [128, 1], F32)
    bias_b2_t = SB(ctx, "bias_b2_t", [128, 16], F32)

    def bias_cols(e, dst, src, n):
        return e.dma_start(out=dst[:, 0:n],
                           in_=bass.AP(src.tensor, 0, [[1, 128], [128, n]]),
                           allow_slow_non_contiguous=True)

    def g_const(e):
        dq(e, "dC", ident_t[:, :], ident[:, :])
        dq(e, "dC", ones_t[:, :], ones_bf[:, :])
        dq(e, "dC", identf_t[:, :], ident_f[:, :])
        dq(e, "dC", onesf_t[:, :], ones_f[:, :])
        bias_cols(e, bias_o_t, b_o, 16).then_inc(sems["dC"], 16)
        bias_cols(e, bias_b2_t, b2, 16).then_inc(sems["dC"], 16)
        return None
    sch.emit("sync", g_const, inc="dC", inc_amt=16 * 6)
    sch.emit("vector", lambda e: (e.memset(eps_t[:, :], 1e-5), e.drain())[-1],
             inc="eV")
    MEMS_G = None
    BAR = sch.barrier()

    s1 = contextlib.ExitStack()
    x_b = [SB(s1, f"s1_x{i}", [128, D], F32) for i in range(2)]
    xb_b = [SB(s1, f"s1_xb{i}", [128, D], BF16) for i in range(2)]
    sq_b = SB(s1, "s1_sq", [128, D], F32)
    sq2_b = SB(s1, "s1_sq2", [128, D], F32)
    z0_pp = SB(s1, "s1_z0", [128, 1], F32)
    d_b = SB(s1, "s1_d", [128, D], F32)
    t1_b = SB(s1, "s1_t1", [128, D], F32)
    h0_b = SB(s1, "s1_h0", [128, D], BF16)
    u_b = SB(s1, "s1_u", [128, D], BF16)
    hq0_b = SB(s1, "s1_hq0", [128, D], BF16)
    trev_b = [SB(s1, f"s1_tr{i}", [128, 512], BF16) for i in range(2)]
    lnw_rep = SB(s1, "s1_lnw", [128, D], F32)
    lnb_rep = SB(s1, "s1_lnb", [128, D], F32)
    st_b = SB(s1, "s1_st", [128, 32], F32)
    am_b = SB(s1, "s1_am", [128, 64], F32)
    sbts = SB(s1, "s1_sbts", [128, 64], I32)
    ibts = SB(s1, "s1_ibts", [128, 64], I32)
    scb_b = SB(s1, "s1_scb", [128, 64], BF16)
    ivb_b = SB(s1, "s1_ivb", [128, 64], BF16)
    p_tr = PS(s1, "s1_ptr", [128, 512], BF16)

    sch.emit("sync", lambda e: (
        dq(e, "dL", lnw_rep[:, :], ln_w.broadcast_to([128, D])),
        dq(e, "dL", lnb_rep[:, :], ln_b.broadcast_to([128, D])))[-1] and None,
        waits=BAR, inc="dL", inc_amt=32)
    LNREP = sch.ms("dL")

    sch.emit("vector", lambda e: (e.memset(z0_pp[:, :], 0.0),
                                  e.memset(st_b[:, :], 0.0), e.drain())[-1],
             inc="eV")
    Z0PP = sch.ms("eV")
    v_done_tile = {}
    g_done_tile = {}
    for i in range(NT):
        xb = x_b[i % 2]
        xbb = xb_b[i % 2]
        w_reuse = [v_done_tile[i - 2]] if i >= 2 else []
        slot = "d%d" % (i % 2)
        sch.emit("sync", (lambda xb=xb, i=i: lambda e:
                 e.dma_start(out=xb[:, :], in_=x_nat[i * 128:(i + 1) * 128, :]))(),
                 waits=w_reuse + (BAR if i < 2 else []), inc=slot, inc_amt=16)
        LD = sch.ms(slot)

        soff = (i % 2) * 16

        def stc(a, b):
            return st_b[:, soff + a:soff + b]

        def sums1(e, xb=xb, stc=stc):
            e.tensor_scalar(out=d_b[:, 0:D], in0=xb[:, :], scalar1=1.0,
                            scalar2=None, op0=Alu.mult, op1=Alu.add,
                            accum_out=stc(0, 1))
            e.drain()
            e.scalar_tensor_tensor(out=sq_b[:, :], in0=xb[:, :], scalar=1.0,
                                   in1=xb[:, :], op0=Alu.mult, op1=Alu.mult,
                                   accum_out=stc(1, 2))
            return e.drain()
        sch.emit("vector", sums1, waits=[LD, Z0PP], inc="eV")
        SUMS1 = sch.ms("eV")

        def v1(e, stc=stc):
            e.tensor_scalar(out=stc(2, 4), in0=stc(0, 2), scalar1=float(1.0 / D),
                            scalar2=None, op0=Alu.mult)
            e.drain()
            e.tensor_tensor(out=stc(4, 5), in0=stc(2, 3), in1=stc(2, 3),
                            op=Alu.mult)
            e.drain()
            e.tensor_tensor(out=stc(5, 6), in0=stc(3, 4), in1=stc(4, 5),
                            op=Alu.subtract)
            e.drain()
            e.tensor_scalar(out=stc(5, 6), in0=stc(5, 6),
                            scalar1=1e-5, scalar2=None, op0=Alu.add)
            return e.drain()
        sch.emit("vector", v1, waits=[SUMS1], inc="eV")
        ST1 = sch.ms("eV")
        sch.emit("scalar", (lambda stc=stc: lambda e:
                 e.activation(out=stc(6, 7), in_=stc(5, 6), func=Act.Sqrt))(),
                 waits=[ST1], inc="eS")
        SD = sch.ms("eS")

        def v2(e, xb=xb, xbb=xbb, stc=stc):
            e.reciprocal(out=stc(7, 8), in_=stc(6, 7))
            e.drain()
            e.tensor_tensor(out=stc(6, 7), in0=stc(7, 8), in1=stc(7, 8),
                            op=Alu.mult)
            e.drain()
            e.tensor_tensor(out=stc(6, 7), in0=stc(6, 7), in1=stc(5, 6),
                            op=Alu.mult)
            e.drain()
            e.tensor_scalar(out=stc(6, 7), in0=stc(6, 7),
                            scalar1=-0.5, scalar2=1.5, op0=Alu.mult, op1=Alu.add)
            e.drain()
            e.tensor_tensor(out=stc(7, 8), in0=stc(7, 8), in1=stc(6, 7),
                            op=Alu.mult)
            e.drain()
            return e.tensor_scalar(out=xbb[:, :], in0=xb[:, :],
                                   scalar1=stc(2, 3), scalar2=stc(7, 8),
                                   op0=Alu.subtract, op1=Alu.mult)
        w_v2 = [SD]
        if sch.cnt.get(("_s1_xbb_free", i - 2)) is not None:
            w_v2.append(sch.cnt[("_s1_xbb_free", i - 2)])
        sch.emit("vector", v2, waits=w_v2, inc="eV")
        XBB = sch.ms("eV")
        def sums2(e, xbb=xbb, stc=stc):
            e.tensor_scalar(out=d_b[:, 0:D], in0=xbb[:, :], scalar1=1.0,
                            scalar2=None, op0=Alu.mult, op1=Alu.add,
                            accum_out=stc(8, 9))
            e.drain()
            e.scalar_tensor_tensor(out=sq_b[:, :], in0=xbb[:, :], scalar=1.0,
                                   in1=xbb[:, :], op0=Alu.mult, op1=Alu.mult,
                                   accum_out=stc(9, 10))
            return e.drain()
        sch.emit("vector", sums2, waits=[XBB], inc="eV")
        SUMS2 = sch.ms("eV")

        def v2b(e, stc=stc):
            e.tensor_scalar(out=stc(10, 12), in0=stc(8, 10),
                            scalar1=float(1.0 / D), scalar2=None, op0=Alu.mult)
            e.drain()
            e.tensor_tensor(out=stc(12, 13), in0=stc(10, 11), in1=stc(10, 11),
                            op=Alu.mult)
            e.drain()
            e.tensor_tensor(out=stc(13, 14), in0=stc(11, 12), in1=stc(12, 13),
                            op=Alu.subtract)
            e.drain()
            e.tensor_scalar(out=stc(13, 14), in0=stc(13, 14),
                            scalar1=1e-5, scalar2=None, op0=Alu.add)
            return e.drain()
        sch.emit("vector", v2b, waits=[SUMS2], inc="eV")
        ST2 = sch.ms("eV")
        sch.emit("scalar", (lambda stc=stc: lambda e:
                 e.activation(out=stc(14, 15), in_=stc(13, 14), func=Act.Sqrt))(),
                 waits=[ST2], inc="eS")
        SD2 = sch.ms("eS")

        def v3(e, xbb=xbb, stc=stc):
            e.reciprocal(out=stc(15, 16), in_=stc(14, 15))
            e.drain()
            e.tensor_tensor(out=stc(14, 15), in0=stc(15, 16),
                            in1=stc(15, 16), op=Alu.mult)
            e.drain()
            e.tensor_tensor(out=stc(14, 15), in0=stc(14, 15),
                            in1=stc(13, 14), op=Alu.mult)
            e.drain()
            e.tensor_scalar(out=stc(14, 15), in0=stc(14, 15),
                            scalar1=-0.5, scalar2=1.5, op0=Alu.mult, op1=Alu.add)
            e.drain()
            e.tensor_scalar(out=stc(15, 16), in0=stc(15, 16),
                            scalar1=stc(14, 15), scalar2=None, op0=Alu.mult)
            e.drain()
            e.tensor_scalar(out=d_b[:, :], in0=xbb[:, :], scalar1=stc(10, 11),
                            scalar2=stc(15, 16), op0=Alu.subtract, op1=Alu.mult)
            e.drain()
            e.tensor_tensor(out=t1_b[:, :], in0=d_b[:, :], in1=lnw_rep[:, :],
                            op=Alu.mult)
            e.drain()
            e.tensor_tensor(out=h0_b[:, :], in0=t1_b[:, :], in1=lnb_rep[:, :],
                            op=Alu.add)
            e.drain()
            return quant_chain(e, h0_b, hq0_b, am_b, sbts, ibts, scb_b, ivb_b, 64, u_b)
        w_v3 = [SD2, LNREP]
        if sch.cnt.get(("_s1_xbb_free", i - 1)) is not None:
            w_v3.append(sch.cnt[("_s1_xbb_free", i - 1)])
        V3 = sch.emit("vector", v3, waits=w_v3, inc="eV")
        v_done_tile[i] = V3

        for (srcname, src, dst) in (("xb", xbb, xbT_s), ("hq0", hq0_b, hq0T_s)):
            for cg in range(4):
                trb = trev_b[cg % 2]
                w_tr = [V3]
                if sch.cnt.get("_s1_last_ev") is not None:
                    w_tr.append(sch.cnt["_s1_last_ev"])

                def mk_tr(src=src, cg=cg):
                    def f(e):
                        last = None
                        for k in range(4):
                            c = cg * 4 + k
                            last = e.transpose(out=p_tr[:, k * 128:(k + 1) * 128],
                                               in_=src[:, c * 128:(c + 1) * 128],
                                               identity=ident_t[:, :])
                        return last
                    return f
                sch.emit("tensor", mk_tr(), waits=w_tr, inc="eP")
                TR = sch.ms("eP")
                w_ev = [TR]
                if sch.cnt.get(("_s1_tr_g", cg % 2)) is not None:
                    w_ev.append(sch.cnt[("_s1_tr_g", cg % 2)])
                def mk_ev(trb=trb):
                    return lambda e: e.copy(out=trb[:, :], in_=p_tr[:, :])
                sch.emit("scalar", mk_ev(), waits=w_ev, inc="eS")
                EV = sch.ms("eS")
                sch.cnt["_s1_last_ev"] = EV

                sslot = "dS1a" if cg % 2 == 0 else "dS1b"
                def mk_st(dst=dst, i=i, cg=cg, trb=trb, sslot=sslot):
                    def f(e):
                        for k in range(4):
                            c = cg * 4 + k
                            dq(e, sslot,
                               dst[c * 128:(c + 1) * 128, i * 128:(i + 1) * 128],
                               trb[:, k * 128:(k + 1) * 128])
                        return None
                    return f
                G = sch.emit("gpsimd", mk_st(), waits=[EV], inc=sslot, inc_amt=64)
                g_done_tile[(i, srcname, cg)] = G
                sch.cnt[("_s1_tr_g", cg % 2)] = G
        sch.cnt[("_s1_xbb_free", i)] = sch.ms("eP")
    s1.close()
    BAR = sch.barrier()

    s2_units = []

    def author_s2_alloc(st):
        g = {}
        g["wnat"] = [SB(st, f"s2_wn{i}", [128, 2048], F32) for i in range(2)]
        g["am2"] = [SB(st, f"s2_am{j}", [128, 128], F32) for j in range(2)]
        g["sb2"] = [SB(st, f"s2_sb{j}", [128, 128], I32) for j in range(2)]
        g["ib2"] = [SB(st, f"s2_ib{j}", [128, 128], I32) for j in range(2)]
        g["scb2"] = [SB(st, f"s2_scb{j}", [128, 128], BF16) for j in range(2)]
        g["ivb2"] = [SB(st, f"s2_ivb{j}", [128, 128], BF16) for j in range(2)]
        g["tra"] = [SB(st, f"s2_tra{j}", [128, 128], BF16) for j in range(2)]
        g["trb"] = [SB(st, f"s2_trb{j}", [128, 128], BF16) for j in range(2)]
        g["wt"] = [SB(st, f"s2_wt{i}", [128, 2048], F32) for i in range(2)]
        g["ive"] = [SB(st, f"s2_ive{i}", [128, 2048], BF16) for i in range(2)]
        g["sce"] = [SB(st, f"s2_sce{i}", [128, 2048], BF16) for i in range(2)]
        g["uq"] = SB(st, "s2_uq", [128, 2048], F32)
        g["rq"] = SB(st, "s2_rq", [128, 2048], BF16)
        g["wq"] = [SB(st, f"s2_wq{i}", [128, 2048], BF16) for i in range(2)]
        return g

    def author_s2_units(g):
        passA = []
        for (wnat, ntile, tcols, scT, ivT, wid) in (
                (W1, 64, 2048, sc1T_s, iv1T_s, "w1"),
                (W2, 16, 8192, sc2T_s, iv2T_s, "w2")):
            nhalf = tcols // 2048 if tcols > 2048 else 1
            hw = min(tcols, 2048)
            nblk = hw // 32
            for b2 in range(nhalf):
                for i in range(ntile):
                    passA.append((wnat, i, b2, hw, nblk, scT, ivT, wid))

        uidx = [0]

        def unit_prologue():
            def mz(e):
                for b in (g["scb2"][0], g["scb2"][1], g["ivb2"][0], g["ivb2"][1]):
                    e.memset(b[:, :], 0.0)
                return e.drain()
            sch.emit("vector", mz, waits=[], inc="eV")
        s2_units.append(unit_prologue)

        def mk_passA(args):
            wnat, i, b2, hw, nblk, scT, ivT, wid = args
            def unit():
                u = uidx[0]; uidx[0] += 1
                par = u % 2
                wb = g["wnat"][par]
                w_ld = [sch.cnt.get(("_s2a_va", par), None)]
                w_ld = [w for w in w_ld if w]
                ldq = "sync" if par == 0 else "scalar"
                ldslot = "dW0" if par == 0 else "dW3"
                sch.emit(ldq, (lambda wb=wb: lambda e: e.dma_start(
                    out=wb[:, 0:hw],
                    in_=wnat[i * 128:(i + 1) * 128, b2 * 2048:b2 * 2048 + hw]))(),
                    waits=w_ld, inc=ldslot, inc_amt=16)
                LD = sch.ms(ldslot)
                am2, sb2, ib2 = g["am2"][par], g["sb2"][par], g["ib2"][par]
                scb2, ivb2 = g["scb2"][par], g["ivb2"][par]

                def va(wb=wb):
                    def f(e):
                        e.reduce_max(out=am2[:, 0:nblk],
                                     in_=wb[:, 0:hw].rearrange("p (nb b) -> p nb b", b=32),
                                     axis=AX, apply_absolute_value=True)
                        e.drain()
                        e.tensor_scalar(out=sb2[:, 0:nblk], in0=am2[:, 0:nblk].bitcast(I32),
                                        scalar1=EXPMASK, scalar2=None, op0=Alu.bitwise_and)
                        e.drain()
                        e.tensor_scalar(out=sb2[:, 0:nblk], in0=sb2[:, 0:nblk],
                                        scalar1=EMIN, scalar2=ESUB,
                                        op0=Alu.max, op1=Alu.subtract)
                        e.drain()
                        e.tensor_scalar(out=ib2[:, 0:nblk], in0=sb2[:, 0:nblk],
                                        scalar1=EINV, scalar2=-1.0,
                                        op0=Alu.subtract, op1=Alu.mult)
                        e.drain()
                        e.tensor_copy(out=scb2[:, 0:nblk], in_=sb2[:, 0:nblk].bitcast(F32))
                        e.drain()
                        return e.tensor_copy(out=ivb2[:, 0:nblk],
                                             in_=ib2[:, 0:nblk].bitcast(F32))
                    return f
                w_va = [LD]
                st_prev = sch.cnt.get(("_s2a_st", par), None)
                if st_prev:
                    w_va.append(st_prev)
                VA = sch.emit("vector", va(), waits=w_va, inc="eV")
                sch.cnt[("_s2a_va", par)] = VA

                rowoff = b2 * nblk
                stq = "sync"
                stslot = "dT0" if par == 0 else "dT1"
                tra, trb = g["tra"][par], g["trb"][par]

                def mk_trs(scb2=scb2, ivb2=ivb2, tra=tra, trb=trb,
                           nblk=nblk, stslot=stslot):
                    def f(e):
                        e.dma_start_transpose(
                            out=tra[:, :],
                            in_=scb2[:, :]).then_inc(sems[stslot], 16)
                        e.dma_start_transpose(
                            out=trb[:, :],
                            in_=ivb2[:, :]).then_inc(sems[stslot], 16)
                        return None
                    return f
                w_tr = [VA]
                st_pr2 = sch.cnt.get(("_s2a_gg", par), None)
                if st_pr2:
                    w_tr.append(st_pr2)
                sch.emit(stq, mk_trs(), waits=w_tr, inc=stslot, inc_amt=32)
                TRS = sch.ms(stslot)
                sch.cnt[("_s2a_st", par)] = TRS
                gslot = "dS2" if par == 0 else "dS2b"

                def mk_sts(tra=tra, trb=trb, scT=scT, ivT=ivT, i=i,
                           rowoff=rowoff, nblk=nblk, gslot=gslot):
                    def f(e):
                        dq(e, gslot,
                           scT[rowoff:rowoff + nblk, i * 128:(i + 1) * 128],
                           tra[0:nblk, 0:128])
                        dq(e, gslot,
                           ivT[rowoff:rowoff + nblk, i * 128:(i + 1) * 128],
                           trb[0:nblk, 0:128])
                        return None
                    return f
                sch.emit("gpsimd", mk_sts(), waits=[TRS], inc=gslot, inc_amt=32)
                sch.cnt[("_s2a_gg", par)] = sch.ms(gslot)
                sch.cnt[("_s2a_done", u)] = sch.ms(gslot)
                prev_other = sch.cnt.get(("_s2a_gg", 1 - par), None)
                if prev_other is not None:
                    sch.cnt[("_s2a_doneo", u)] = prev_other
            return unit

        for args in passA:
            s2_units.append(mk_passA(args))

        passB = []
        for (wt, ntile, width, nq, scT, ivT, wqT, wid) in (
                (W1T, 16, DF, 4, sc1T_s, iv1T_s, W1R_s, "w1"),
                (W2T, 64, D, 1, sc2T_s, ivT2 if False else iv2T_s, W2R_s, "w2")):
            for qo in range(nq):
                for i in range(ntile):
                    passB.append((wt, i, qo, width, scT, ivT, wqT, wid))

        ubidx = [0]

        def mk_passB(args):
            wt, i, qo, width, scT, ivT, wqT, wid = args
            def unit():
                u = ubidx[0]; ubidx[0] += 1
                par = u % 2
                wtb, wqb = g["wt"][par], g["wq"][par]
                w_ld = [sch.cnt.get(("_s2b_vq", par), None)]
                w_ld = [w for w in w_ld if w]
                if wid == "w1":
                    gA_hi = qo * 16 + 15
                else:
                    gA_hi = 64 + (i // 16) * 16 + 15
                for k in (gA_hi, gA_hi - 1):
                    gd = sch.cnt.get(("_s2a_done", k), None)
                    if gd:
                        w_ld.append(gd)
                    gdo = sch.cnt.get(("_s2a_doneo", k), None)
                    if gdo:
                        w_ld.append(gdo)
                ldq = "sync" if par == 0 else "scalar"
                ldslot = "dW1" if par == 0 else "dW0"
                sch.emit(ldq, (lambda wtb=wtb: lambda e: e.dma_start(
                    out=wtb[:, :],
                    in_=wt[i * 128:(i + 1) * 128,
                           qo * 2048:(qo + 1) * 2048]))(),
                    waits=w_ld, inc=ldslot, inc_amt=16)
                LD = sch.ms(ldslot)

                expslot = "dW2" if par == 0 else "dW3"
                ivb, scb = g["ive"][par], g["sce"][par]

                def mk_exp(ivb=ivb, scb=scb, expslot=expslot):
                    def f(e):
                        for gg in range(4):
                            dq(e, expslot, ivb[32 * gg:32 * gg + 32, :],
                               ivT[4 * i + gg:4 * i + gg + 1,
                                   qo * 2048:(qo + 1) * 2048].broadcast_to([32, 2048]))
                            dq(e, expslot, scb[32 * gg:32 * gg + 32, :],
                               scT[4 * i + gg:4 * i + gg + 1,
                                   qo * 2048:(qo + 1) * 2048].broadcast_to([32, 2048]))
                        return None
                    return f
                w_exp = [sch.cnt.get(("_s2b_vq", par), None)]
                w_exp = [w for w in w_exp if w]
                for k in (gA_hi, gA_hi - 1):
                    gd = sch.cnt.get(("_s2a_done", k), None)
                    if gd:
                        w_exp.append(gd)
                    gdo = sch.cnt.get(("_s2a_doneo", k), None)
                    if gdo:
                        w_exp.append(gdo)
                expq = "scalar" if par == 0 else "sync"
                sch.emit(expq, mk_exp(), waits=w_exp, inc=expslot, inc_amt=16 * 8)
                EXP = sch.ms(expslot)

                def vq(wtb=wtb, wqb=wqb, ivb=ivb, scb=scb):
                    def f(e):
                        e.tensor_tensor(out=g["uq"][:, :], in0=wtb[:, :],
                                        in1=ivb[:, :], op=Alu.mult)
                        e.drain()
                        e.tensor_scalar(out=g["rq"][:, :], in0=g["uq"][:, :],
                                        scalar1=MAGIC, scalar2=MAGIC,
                                        op0=Alu.add, op1=Alu.subtract)
                        e.drain()
                        return e.scalar_tensor_tensor(
                            out=wqb[:, :], in0=g["rq"][:, :], scalar=127.0,
                            in1=scb[:, :], op0=Alu.min, op1=Alu.mult)
                    return f
                w_vq = [LD, EXP]
                g_prev = sch.cnt.get(("_s2b_g", par), None)
                if g_prev:
                    w_vq.append(g_prev)
                VQ = sch.emit("vector", vq(), waits=w_vq, inc="eV")
                sch.cnt[("_s2b_vq", par)] = VQ

                wqslot = "dS2q" if par == 0 else "dS2qb"
                if wid == "w1":
                    st_ap = bass.AP(wqT.tensor, qo * 32768 + i * 512,
                                    [[NFC * DF, 128], [8192, 4], [1, 512]])
                else:
                    st_ap = bass.AP(wqT.tensor, i * 128,
                                    [[64 * D, 128], [8192, 16], [1, 128]])
                sch.emit("gpsimd", (lambda wqb=wqb, st_ap=st_ap: lambda e:
                    e.dma_start(out=st_ap, in_=wqb[:, :]))(),
                    waits=[VQ], inc=wqslot, inc_amt=16)
                sch.cnt[("_s2b_g", par)] = sch.ms(wqslot)
            return unit

        for args in passB:
            s2_units.append(mk_passB(args))

    BAR = sch.barrier()

    s3keep = contextlib.ExitStack()
    K_t = SB(s3keep, "s3_K", [128, NFC * S], BF16)
    qlatT_t = SB(s3keep, "s3_qlatT", [128, 4 * S], BF16)
    s3 = contextlib.ExitStack()
    xbT_full = SB(s3, "s3_xbTf", [128, NFC * 2048], BF16)
    ckvT_t = SB(s3, "s3_ckvT", [128, 4 * S], BF16)
    wbuf = SB(s3, "s3_wbuf", [128, 16384], BF16)
    bias_kv_t = SB(s3, "s3_bkv", [128, 4], F32)
    bias_q_t = SB(s3, "s3_bq", [128, 4], F32)
    bias_ku_t = SB(s3, "s3_bku", [128, 8], F32)
    bias_kr_t = SB(s3, "s3_bkr", [128, 8], F32)
    bvu_rep = SB(s3, "s3_bvu", [128, D], F32)
    v_t = SB(s3, "s3_v", [128, D], BF16)
    p_a = [PS(s3, f"s3_pa{i}", [128, 2048], F32) for i in range(2)]

    def s3_bias(e):
        bias_cols(e, bias_kv_t, b_kv, 4).then_inc(sems["d3"], 16)
        bias_cols(e, bias_q_t, b_q, 4).then_inc(sems["d3"], 16)
        bias_cols(e, bias_ku_t, b_ku, 8).then_inc(sems["d3"], 16)
        bias_cols(e, bias_kr_t, b_kr, 8).then_inc(sems["d3"], 16)
        dq(e, "d3", bvu_rep[:, :], b_vu.broadcast_to([128, D]))
        return None
    sch.emit("sync", s3_bias, waits=BAR, inc="d3", inc_amt=16 * 5)
    S3BIAS = sch.ms("d3")

    def ld_xbT(e):
        for c in range(NFC):
            dq(e, "d4", xbT_full[:, c * 2048:(c + 1) * 2048],
               xbT_s[c * 128:(c + 1) * 128, :])
        return None
    sch.emit("sync", ld_xbT, waits=BAR, inc="d4", inc_amt=16 * NFC)
    XBT_LD = sch.ms("d4")

    def proj(dst, wsrc, wcols, bias_t, nout, name):
        sch.emit("gpsimd", (lambda: lambda e:
                 e.dma_start(out=wbuf[:, 0:NFC * wcols],
                             in_=w_chunked(wsrc, wcols, NFC)))(),
                 waits=[XBT_LD, sch.ms("eP")], inc="gA", inc_amt=16)
        WLD = sch.ms("gA")
        for oc in range(nout):
            ps = p_a[oc % 2]
            w_ps = [sch.cnt.get(("pa_ev", oc % 2), None)]
            w_ps = [w for w in w_ps if w]

            def mk_mm(oc=oc, wcols=wcols, ps=ps):
                def f(e):
                    last = None
                    for c in range(NFC):
                        for j in range(4):
                            last = e.matmul(
                                out=ps[:, j * 512:(j + 1) * 512],
                                lhsT=wbuf[:, c * wcols + oc * 128:
                                          c * wcols + (oc + 1) * 128],
                                rhs=xbT_full[:, c * 2048 + j * 512:
                                             c * 2048 + (j + 1) * 512],
                                start=(c == 0), stop=(c == NFC - 1))
                    return last
                return f
            sch.emit("tensor", mk_mm(), waits=[WLD] + w_ps, inc="eP")
            MM = sch.ms("eP")

            def mk_ev(dst=dst, oc=oc, bias_t=bias_t, ps=ps):
                return lambda e: e.activation(out=dst[:, oc * 2048:(oc + 1) * 2048],
                                              in_=ps[:, :], func=Act.Identity,
                                              bias=bias_t[:, oc:oc + 1], scale=1.0)
            E = sch.emit("scalar", mk_ev(), waits=[MM, S3BIAS], inc="eS")
            sch.cnt[(name, oc)] = E
            sch.cnt[("pa_ev", oc % 2)] = E

    proj(ckvT_t, w_kv, DL, bias_kv_t, 4, "ckv")
    proj(qlatT_t, w_q, DL, bias_q_t, 4, "qlat")
    sch.emit("gpsimd", (lambda: lambda e:
             e.dma_start(out=wbuf[:, 0:NFC * 1024],
                         in_=w_chunked(w_kr, 1024, NFC)))(),
             waits=[XBT_LD, sch.ms("eP")], inc="gA", inc_amt=16)
    WLD = sch.ms("gA")
    for oc in range(8):
        ps = p_a[oc % 2]
        w_ps = [sch.cnt.get(("pa_ev", oc % 2), None)]
        w_ps = [w for w in w_ps if w]

        def mk_mm(oc=oc, ps=ps):
            def f(e):
                last = None
                for c in range(NFC):
                    for j in range(4):
                        last = e.matmul(
                            out=ps[:, j * 512:(j + 1) * 512],
                            lhsT=wbuf[:, c * 1024 + oc * 128:c * 1024 + (oc + 1) * 128],
                            rhs=xbT_full[:, c * 2048 + j * 512:c * 2048 + (j + 1) * 512],
                            start=(c == 0), stop=(c == NFC - 1))
                return last
            return f
        sch.emit("tensor", mk_mm(), waits=[WLD] + w_ps, inc="eP")
        MM = sch.ms("eP")

        def mk_ev(oc=oc, ps=ps):
            return lambda e: e.activation(
                out=K_t[:, (8 + oc) * 2048:(9 + oc) * 2048], in_=ps[:, :],
                func=Act.Identity, bias=bias_kr_t[:, oc:oc + 1], scale=1.0)
        E = sch.emit("scalar", mk_ev(), waits=[MM, S3BIAS], inc="eS")
        sch.cnt[("kr", oc)] = E
        sch.cnt[("pa_ev", oc % 2)] = E
    KR_DONE = sch.barrier()

    sch.emit("gpsimd", (lambda: lambda e:
             e.dma_start(out=wbuf[:, 0:4 * 1024],
                         in_=w_chunked(w_ku, 1024, 4)))(),
             waits=KR_DONE, inc="gA", inc_amt=16)
    WLD = sch.ms("gA")
    for oc in range(8):
        ps = p_a[oc % 2]
        w_ps = [sch.cnt.get(("pa_ev", oc % 2), None)]
        w_ps = [w for w in w_ps if w]

        def mk_mm(oc=oc, ps=ps):
            def f(e):
                last = None
                for c in range(4):
                    for j in range(4):
                        last = e.matmul(
                            out=ps[:, j * 512:(j + 1) * 512],
                            lhsT=wbuf[:, c * 1024 + oc * 128:c * 1024 + (oc + 1) * 128],
                            rhs=ckvT_t[:, c * 2048 + j * 512:c * 2048 + (j + 1) * 512],
                            start=(c == 0), stop=(c == 3))
                return last
            return f
        sch.emit("tensor", mk_mm(), waits=[WLD] + w_ps, inc="eP")
        MM = sch.ms("eP")

        def mk_ev(oc=oc, ps=ps):
            return lambda e: e.activation(
                out=K_t[:, oc * 2048:(oc + 1) * 2048], in_=ps[:, :],
                func=Act.Identity, bias=bias_ku_t[:, oc:oc + 1], scale=1.0)
        E = sch.emit("scalar", mk_ev(), waits=[MM, S3BIAS], inc="eS")
        sch.cnt[("ku", oc)] = E
        sch.cnt[("pa_ev", oc % 2)] = E

    sch.emit("gpsimd", (lambda: lambda e:
             e.dma_start(out=wbuf[:, 4096:4096 + 4 * 2048],
                         in_=w_chunked(w_vu, 2048, 4)))(),
             waits=KR_DONE, inc="gB", inc_amt=16)
    WVLD = sch.ms("gB")
    for tt_ in range(NT):
        ps = p_a[tt_ % 2]
        w_ps = [sch.cnt.get(("pa_ev", tt_ % 2), None)]
        w_ps = [w for w in w_ps if w]

        def mk_mm(tt_=tt_, ps=ps):
            def f(e):
                last = None
                for c in range(4):
                    for j in range(4):
                        last = e.matmul(
                            out=ps[:, j * 512:(j + 1) * 512],
                            lhsT=ckvT_t[:, c * 2048 + tt_ * 128:c * 2048 + (tt_ + 1) * 128],
                            rhs=wbuf[:, 4096 + c * 2048 + j * 512:
                                     4096 + c * 2048 + (j + 1) * 512],
                            start=(c == 0), stop=(c == 3))
                return last
            return f
        sch.emit("tensor", mk_mm(), waits=[WVLD] + w_ps, inc="eP")
        MM = sch.ms("eP")

        def mk_ev(tt_=tt_, ps=ps):
            def f(e):
                return e.tensor_tensor(out=v_t[:, :], in0=ps[:, :],
                                       in1=bvu_rep[:, :], op=Alu.add)
            return f
        w_evv = [MM, S3BIAS]
        vst_prev = sch.cnt.get(("v", tt_ - 1), None)
        if vst_prev:
            w_evv.append(vst_prev)
        EVv = sch.emit("vector", mk_ev(), waits=w_evv, inc="eV")
        sch.cnt[("pa_ev", tt_ % 2)] = EVv
        EV = sch.ms("eV")
        G = sch.emit("gpsimd", (lambda tt_=tt_: lambda e:
                     e.dma_start(out=V_s[tt_ * 128:(tt_ + 1) * 128, :], in_=v_t[:, :]))(),
                     waits=[EV], inc="dS3", inc_amt=16)
        sch.cnt[("v", tt_)] = G
    s3.close()
    BAR = sch.barrier()

    s4 = contextlib.ExitStack()
    wqu_t = SB(s4, "s4_wqu", [128, 4 * 1024], BF16)
    wqr_t = SB(s4, "s4_wqr", [128, 4 * 1024], BF16)
    bias_qu_t = SB(s4, "s4_bqu", [128, 8], F32)
    bias_qr_t = SB(s4, "s4_bqr", [128, 8], F32)
    qt_h2 = [SB(s4, f"s4_qt{i}", [128, S], BF16) for i in range(2)]
    vh_t2 = [SB(s4, f"s4_vh{i}", [128, NT * 128], BF16) for i in range(2)]
    pt_b = [SB(s4, f"s4_pt{i}", [128, 512], BF16) for i in range(2)]
    nrow = SB(s4, "s4_nrow", [1, 256], F32)
    negb2 = [SB(s4, f"s4_negb{i}", [128, 1], F32) for i in range(2)]
    rz_b = SB(s4, "s4_rz", [128, 16], F32)
    zrow_b = [SB(s4, f"s4_zrow{i}", [1, 512], F32) for i in range(2)]
    ac_hb = [SB(s4, f"s4_achb{i}", [128, 512], BF16) for i in range(2)]
    ac_ev = [SB(s4, f"s4_acev{i}", [128, 512], BF16) for i in range(2)]
    p_q = PS(s4, "s4_pq", [128, 1024], F32)
    p_sc = [PS(s4, f"s4_psc{i}", [128, 512], F32) for i in range(3)]
    p_pvT = [PS(s4, f"s4_ppv{i}", [128, 512], F32) for i in range(2)]
    p_tb = PS(s4, "s4_ptb", [128, 1024], BF16)
    maskslab_t = SB(s4, "s4_maskslab", [128, 2048], BF16)
    zrow0_t = SB(s4, "s4_zeror", [1, 512], BF16)
    g2 = author_s2_alloc(s4)
    g2["ptr2"] = p_tb
    sch.emit("sync", lambda e: (
        dq(e, "dC", maskslab_t[:, :], maskslab[:, :]),)[-1] and None,
        waits=BAR, inc="dC", inc_amt=16)
    MSK_LD = sch.ms("dC")
    sch.emit("vector", lambda e: (e.memset(zrow0_t[:, :], 0.0), e.drain())[-1],
             waits=BAR, inc="eV")
    Z0_RDY = sch.ms("eV")
    author_s2_units(g2)
    s2_drained = [0]

    def drain_s2(n):
        k = 0
        while s2_units and k < n:
            s2_units.pop(0)()
            k += 1
            s2_drained[0] += 1

    sch.emit("gpsimd", (lambda: lambda e: (
        dq(e, "gC", wqu_t[:, :], w_chunked(w_qu, 1024, 4)),
        dq(e, "gC", wqr_t[:, :], w_chunked(w_qr, 1024, 4)))[-1] and None)(),
        waits=BAR, inc="gC", inc_amt=32)
    def s4_bias(e):
        bias_cols(e, bias_qu_t, b_qu, 8).then_inc(sems["d2"], 16)
        bias_cols(e, bias_qr_t, b_qr, 8).then_inc(sems["d2"], 16)
        return None
    sch.emit("sync", s4_bias, waits=BAR, inc="d2", inc_amt=32)
    QW_LD = [sch.ms("gC"), sch.ms("d2")]

    sqn_b = SB(s4, "s4_sqn", [128, S], BF16)
    mK2row = SB(s4, "s4_mK2row", [1, NH], F32)
    ztp_b = SB(s4, "s4_ztp", [128, 16], F32)
    KN_DONE = None
    for h in range(NH):
        w_sq = [BAR[0]] if h == 0 else [sch.cnt[("kn_red", h - 1)]]
        sch.emit("vector", (lambda h=h: lambda e:
                 e.tensor_tensor(out=sqn_b[:, :],
                                 in0=K_t[:, h * 2048:(h + 1) * 2048],
                                 in1=K_t[:, h * 2048:(h + 1) * 2048],
                                 op=Alu.mult))(),
                 waits=BAR + w_sq, inc="eV")
        KSQ = sch.ms("eV")
        KEV = None
        for j in range(4):
            bk = (j % 2) * 512
            w_nm = [KSQ]
            pv_ev = sch.cnt.get(("kn_evj", h, j - 2), None) or \
                    (sch.cnt.get(("kn_evj", h - 1, j + 2), None) if h > 0 else None)
            if pv_ev:
                w_nm.append(pv_ev)
            sch.emit("tensor", (lambda j=j, bk=bk: lambda e: e.matmul(
                out=p_q[0:1, bk:bk + 512], lhsT=ones_t[:, 0:1],
                rhs=sqn_b[:, j * 512:(j + 1) * 512], start=True, stop=True))(),
                waits=w_nm, inc="eP")
            KNM = sch.ms("eP")
            KEV = sch.emit("vector", (lambda j=j, bk=bk: lambda e:
                           e.reduce_max(out=nrow[0:1, 132 + j:133 + j],
                                        in_=p_q[0:1, bk:bk + 512], axis=AX))(),
                           waits=[KNM], inc="eV")
            sch.cnt[("kn_evj", h, j)] = KEV
        sch.cnt[("kn_ev", h)] = KEV
        KRD = sch.emit("vector", (lambda h=h: lambda e: (
            e.drain(),
            e.reduce_max(out=mK2row[0:1, h:h + 1], in_=nrow[0:1, 132:136],
                         axis=AX))[-1])(), waits=[KEV], inc="eV")
        sch.cnt[("kn_red", h)] = KRD
        KN_DONE = KRD

    JW = (2, 3, 5, 6)

    def emit_fin(h_, j_, negb_):
        def mk_ztp(j_=j_):
            def f(e):
                last = None
                for t4 in range(4):
                    last = e.transpose(
                        out=p_q[:, 17 + j_ * 4 + t4:18 + j_ * 4 + t4],
                        in_=zrow_b[j_ % 2][0:1, t4 * 128:(t4 + 1) * 128],
                        identity=identf_t[0:1, 0:1])
                return last
            return f
        w_ztp = [sch.cnt[("s4zev", h_, j_)]]
        sch.emit("tensor", mk_ztp(), waits=w_ztp, inc="eP")
        ZTP = sch.ms("eP")
        sch.cnt[("s4ztp", h_, j_)] = ZTP
        w_ze = [ZTP]
        if j_ == 0 and h_ > 0:
            w_ze.append(sch.cnt[("s4zrc", h_ - 1, 3)])
        ZTPE = sch.emit("scalar", (lambda j_=j_: lambda e:
                        e.copy(out=ztp_b[:, j_ * 4:(j_ + 1) * 4],
                               in_=p_q[:, 17 + j_ * 4:17 + (j_ + 1) * 4]))(),
                        waits=w_ze, inc="eS")
        sch.cnt[("s4ztpe", h_)] = ZTPE
        w_rc = [ZTPE]
        if j_ == 0 and h_ > 0:
            w_rc.append(sch.cnt[("s4pve", h_ - 1, 3)])
        sch.emit("vector", (lambda j_=j_: lambda e:
                 e.reciprocal(out=rz_b[:, j_ * 4:(j_ + 1) * 4],
                              in_=ztp_b[:, j_ * 4:(j_ + 1) * 4]))(),
                 waits=w_rc, inc="eV")
        ZRC = sch.ms("eV")
        sch.cnt[("s4zrc", h_, j_)] = ZRC
        w_scl = [ZRC]
        tb_pr = sch.cnt.get(("s4tb", h_, j_ - 2), None) or \
                (sch.cnt.get(("s4tb", h_ - 1, j_ + 2), None) if h_ > 0 else None)
        if tb_pr:
            w_scl.append(tb_pr)

        def mk_scl(j_=j_):
            def f(e):
                last = None
                for t4 in range(4):
                    last = e.activation(
                        out=ac_hb[j_ % 2][:, t4 * 128:(t4 + 1) * 128],
                        in_=p_pvT[j_ % 2][:, t4 * 128:(t4 + 1) * 128],
                        func=Act.Identity,
                        scale=rz_b[:, j_ * 4 + t4:j_ * 4 + t4 + 1])
                return last
            return f
        PVE = sch.emit("scalar", mk_scl(), waits=w_scl, inc="eS")
        sch.cnt[("s4pve", h_, j_)] = PVE

    def emit_tb(h_, j_):
        w_tb = [sch.cnt[("s4pve", h_, j_)]]
        tbe_pr = sch.cnt.get(("s4tbe", h_, j_ - 1), None) or \
                 (sch.cnt.get(("s4tbe", h_ - 1, 3), None)
                  if h_ > 0 and j_ == 0 else None)
        if tbe_pr:
            w_tb.append(tbe_pr)

        def mk_tb(h_=h_, j_=j_):
            def f(e):
                last = None
                for t4 in range(4):
                    last = e.transpose(
                        out=p_tb[:, 512 + t4 * 128:512 + (t4 + 1) * 128],
                        in_=ac_hb[j_ % 2][:, t4 * 128:(t4 + 1) * 128],
                        identity=ident_t[:, :])
                return last
            return f
        sch.emit("tensor", mk_tb(), waits=w_tb, inc="eP")
        TB = sch.ms("eP")
        sch.cnt[("s4tb", h_, j_)] = TB
        w_te = [TB]
        st_pr = sch.cnt.get(("s4st", h_, j_ - 2), None) or \
                (sch.cnt.get(("s4st", h_ - 1, j_ + 2), None) if h_ > 0 else None)
        if st_pr:
            w_te.append(st_pr)
        TBE = sch.emit("scalar", (lambda h_=h_, j_=j_: lambda e:
                       e.copy(out=ac_ev[j_ % 2][:, :],
                              in_=p_tb[:, 512:1024]))(),
                       waits=w_te, inc="eS")
        sch.cnt[("s4tbe", h_, j_)] = TBE
        slot4 = "dS4" if j_ % 2 == 0 else "dS4b"
        sch.emit("gpsimd", (lambda h_=h_, j_=j_: lambda e:
                 e.dma_start(out=acT_s[h_ * 128:(h_ + 1) * 128,
                                       j_ * 512:(j_ + 1) * 512],
                             in_=ac_ev[j_ % 2][:, :]))(),
                 waits=[TBE], inc=slot4, inc_amt=16)
        sch.cnt[("s4st", h_, j_)] = sch.ms(slot4)

    for h in range(NH):
        par = h % 2
        wsrc = wqu_t if h < 8 else wqr_t
        hb = h if h < 8 else h - 8
        bsrc = bias_qu_t if h < 8 else bias_qr_t
        qt = qt_h2[par]
        vh = vh_t2[par]

        w_qgen = list(QW_LD)
        if h == 0:
            w_qgen.append(KN_DONE)
        else:
            w_qgen.append(sch.cnt[("s4ztpe", h - 1)])
        sd = sch.cnt.get(("s4scdone", h - 2), None)
        if sd:
            w_qgen.append(sd)
        QEV = None
        for qh2 in range(2):
            def mk_qmm(wsrc=wsrc, hb=hb, qh2=qh2):
                def f(e):
                    last = None
                    for c in range(4):
                        for j2 in range(2):
                            last = e.matmul(
                                out=p_q[:, j2 * 512:(j2 + 1) * 512],
                                lhsT=wsrc[:, c * 1024 + hb * 128:c * 1024 + (hb + 1) * 128],
                                rhs=qlatT_t[:, c * 2048 + qh2 * 1024 + j2 * 512:
                                            c * 2048 + qh2 * 1024 + (j2 + 1) * 512],
                                start=(c == 0), stop=(c == 3))
                    return last
                return f
            w_qm = w_qgen if qh2 == 0 else [sch.ms("eS")]
            sch.emit("tensor", mk_qmm(), waits=w_qm, inc="eP")
            QMM = sch.ms("eP")
            sch.emit("scalar", (lambda bsrc=bsrc, hb=hb, qh2=qh2, qt=qt: lambda e:
                     e.activation(out=qt[:, qh2 * 1024:(qh2 + 1) * 1024],
                                  in_=p_q[:, :], func=Act.Identity,
                                  bias=bsrc[:, hb:hb + 1], scale=1.0))(),
                     waits=[QMM], inc="eS")
            QEV = sch.ms("eS")

        if h > 0:
            emit_tb(h - 1, 3)

        w_sq4 = [QEV]
        qn_prev = sch.cnt.get(("s4qnmm", h - 1), None)
        if qn_prev:
            w_sq4.append(qn_prev)
        sch.emit("vector", (lambda qt=qt: lambda e:
                 e.tensor_tensor(out=sqn_b[:, :], in0=qt[:, :], in1=qt[:, :],
                                 op=Alu.mult))(),
                 waits=w_sq4, inc="eV")
        QSQ = sch.ms("eV")

        QN3 = None
        for j in range(4):
            bk = (j % 2) * 512
            w_nm = [QSQ]
            pv_ev = sch.cnt.get(("qn_evj", h, j - 2), None)
            if pv_ev:
                w_nm.append(pv_ev)
            sch.emit("tensor", (lambda j=j, bk=bk: lambda e: e.matmul(
                out=p_q[0:1, bk:bk + 512], lhsT=ones_t[:, 0:1],
                rhs=sqn_b[:, j * 512:(j + 1) * 512], start=True, stop=True))(),
                waits=w_nm, inc="eP")
            QNM = sch.ms("eP")
            w_cp = [QNM]
            if j == 0:
                kn_pr = sch.cnt.get(("s4qn4", h - 1), None) or KN_DONE
                w_cp.append(kn_pr)
            QN3 = sch.emit("vector", (lambda j=j, bk=bk: lambda e:
                           e.reduce_max(out=nrow[0:1, 136 + j:137 + j],
                                        in_=p_q[0:1, bk:bk + 512], axis=AX))(),
                           waits=w_cp, inc="eV")
            sch.cnt[("qn_evj", h, j)] = QN3
        sch.cnt[("s4qnmm", h)] = QNM

        w_qn4 = [QN3, KN_DONE]
        nb_prev = sch.cnt.get(("s4nb1", h - 1), None)
        if nb_prev:
            w_qn4.append(nb_prev)

        def mk_qn4(h=h):
            def f(e):
                e.drain()
                e.reduce_max(out=nrow[0:1, 128:129], in_=nrow[0:1, 136:140],
                             axis=AX)
                e.drain()
                e.tensor_tensor(out=nrow[0:1, 129:130],
                                in0=nrow[0:1, 128:129],
                                in1=mK2row[0:1, h:h + 1], op=Alu.mult)
                e.drain()
                e.tensor_scalar(out=nrow[0:1, 131:132].bitcast(I32),
                                in0=nrow[0:1, 129:130].bitcast(I32),
                                scalar1=EXPMASK, scalar2=None,
                                op0=Alu.bitwise_and)
                e.drain()
                e.tensor_scalar(out=nrow[0:1, 131:132].bitcast(I32),
                                in0=nrow[0:1, 131:132].bitcast(I32),
                                scalar1=0.5, scalar2=float(64 << 23),
                                op0=Alu.mult, op1=Alu.add)
                e.drain()
                return e.tensor_scalar(out=nrow[0:1, 130:131],
                                       in0=nrow[0:1, 131:132],
                                       scalar1=float(-SM_SCALE), scalar2=None,
                                       op0=Alu.mult)
            return f
        QN4 = sch.emit("vector", mk_qn4(), waits=w_qn4, inc="eV")
        sch.cnt[("s4qn4", h)] = QN4
        QN5 = QN4
        sch.emit("tensor", lambda e: e.matmul(out=p_q[:, 16:17],
                                              lhsT=onesf_t[0:1, :],
                                              rhs=nrow[0:1, 130:131],
                                              start=True, stop=True),
                 waits=[QN5], inc="eP")
        NB1 = sch.ms("eP")
        sch.cnt[("s4nb1", h)] = NB1
        negb = negb2[par]
        sch.emit("scalar", (lambda negb=negb: lambda e: (
                 e.copy(out=negb[:, 0:1], in_=p_q[:, 16:17]),
                 e.drain())[-1])(),
                 waits=[NB1], inc="eS")
        NB = sch.ms("eS")

        def mk_vld(h=h, vh=vh):
            return lambda e: e.dma_start(
                out=vh[:, :],
                in_=bass.AP(V_s.tensor, h * 128,
                            [[D, 128], [128 * D, NT], [1, 128]]))
        w_vld = list(BAR)
        if sd:
            w_vld.append(sd)
        sch.emit("sync", mk_vld(), waits=w_vld, inc="d4", inc_amt=16)
        VLD = sch.ms("d4")

        for j in range(4):
            q0 = j * 512
            kt_hi = 4 * j + 4
            pvb = p_pvT[j % 2]
            EXP_ = None
            for kt in range(kt_hi):
                t = kt - 4 * j
                psc = p_sc[kt % 3]
                ptb = pt_b[kt % 2]
                w_sc = [QEV]
                pr = sch.cnt.get(("s4exp", h, j, kt - 3), None)
                if pr:
                    w_sc.append(pr)
                if kt < 3:
                    ex_pr = sch.cnt.get(("s4expl", h, j - 1), None) or \
                            (sch.cnt.get(("s4expl", h - 1, 3), None)
                             if h > 0 and j == 0 else None)
                    if ex_pr:
                        w_sc.append(ex_pr)

                def mk_sc(h=h, kt=kt, q0=q0, t=t, psc=psc, qt=qt):
                    def f(e):
                        last = e.matmul(
                            out=psc[:, :],
                            lhsT=K_t[:, h * 2048 + kt * 128:h * 2048 + (kt + 1) * 128],
                            rhs=qt[:, q0:q0 + 512],
                            start=True, stop=(t < 0))
                        if t >= 0:
                            last = e.matmul(
                                out=psc[:, :],
                                lhsT=ident_t[:, :],
                                rhs=maskslab_t[:, t * 512:(t + 1) * 512],
                                start=False, stop=True)
                        return last
                    return f
                w_sc.append(MSK_LD)
                sch.emit("tensor", mk_sc(), waits=w_sc, inc="eP")
                SC = sch.ms("eP")

                w_exp = [SC]
                pv_prev = sch.cnt.get(("s4sc", h, j, kt - 2), None)
                if pv_prev:
                    w_exp.append(pv_prev)

                def mk_exp(psc=psc, ptb=ptb, negb=negb):
                    return lambda e: e.activation(
                        out=ptb[:, :], in_=psc[:, :],
                        func=Act.Exp, bias=negb[:, 0:1], scale=SM_SCALE)
                sch.emit("scalar", mk_exp(), waits=w_exp, inc="eS")
                EXP_ = sch.ms("eS")
                sch.cnt[("s4exp", h, j, kt)] = EXP_

                w_pv = [EXP_, VLD]
                if kt == 0:
                    pve_prev = sch.cnt.get(("s4pve", h, j - 2), None) or \
                               (sch.cnt.get(("s4pve", h - 1, j + 2), None)
                                if h > 0 else None)
                    if pve_prev:
                        w_pv.append(pve_prev)
                    zev_prev = sch.cnt.get(("s4zev", h, j - 1), None) or \
                               (sch.cnt.get(("s4zev", h - 1, 3), None)
                                if h > 0 and j == 0 else None)
                    if zev_prev:
                        w_pv.append(zev_prev)
                    w_pv.append(Z0_RDY)

                def mk_pv(kt=kt, ptb=ptb, kt_hi=kt_hi, pvb=pvb, vh=vh):
                    def f(e):
                        st_ = (kt == 0)
                        sp_ = (kt == kt_hi - 1)
                        if st_:
                            e.matmul(out=pvb[:, :], lhsT=zrow0_t[0:1, 0:128],
                                     rhs=zrow0_t[0:1, 0:512],
                                     start=True, stop=False)
                        for t4 in range(4):
                            e.matmul(out=pvb[:, t4 * 128:(t4 + 1) * 128],
                                     lhsT=ptb[:, t4 * 128:(t4 + 1) * 128],
                                     rhs=vh[:, kt * 128:(kt + 1) * 128],
                                     start=False, stop=False)
                        if sp_:
                            e.matmul(out=pvb[:, :], lhsT=zrow0_t[0:1, 0:128],
                                     rhs=zrow0_t[0:1, 0:512],
                                     start=False, stop=True)
                        return e.matmul(out=p_q[0:1, 512:1024],
                                        lhsT=ones_t[:, 0:1], rhs=ptb[:, :],
                                        start=st_, stop=sp_)
                    return f
                PV = sch.emit("tensor", mk_pv(), waits=w_pv, inc="eP")
                sch.cnt[("s4sc", h, j, kt)] = PV
            sch.cnt[("s4expl", h, j)] = EXP_

            w_zev = [sch.ms("eP")]
            zt_pr = sch.cnt.get(("s4ztp", h, j - 2), None) or \
                    (sch.cnt.get(("s4ztp", h - 1, j + 2), None) if h > 0 else None)
            if zt_pr:
                w_zev.append(zt_pr)
            ZEV = sch.emit("scalar", (lambda j=j: lambda e:
                           e.copy(out=zrow_b[j % 2][0:1, :],
                                  in_=p_q[0:1, 512:1024]))(),
                           waits=w_zev, inc="eS")
            sch.cnt[("s4zev", h, j)] = ZEV

            if j >= 1:
                emit_fin(h, j - 1, negb)
            if j >= 2:
                emit_tb(h, j - 2)
            if h == NH - 1 and j == 3:
                drain_s2(len(s2_units))
            else:
                wleft = sum(JW[jj] for jj in range(j + 1, 4)) + \
                        16 * (NH - 1 - h)
                drain_s2(min(len(s2_units),
                             (len(s2_units) * JW[j] + wleft) // (JW[j] + wleft)
                             if wleft > 0 else len(s2_units)))

        emit_fin(h, 3, negb)
        emit_tb(h, 2)
        sch.cnt[("s4scdone", h)] = sch.ms("eP")
    emit_tb(NH - 1, 3)
    drain_s2(len(s2_units))
    s4.close()
    s3keep.close()
    BAR = sch.barrier()

    s5 = contextlib.ExitStack()
    b1sl_b = [SB(s5, f"s5_b1sl{i}", [1, 512], F32) for i in range(2)]
    hq0T_q = SB(s5, "s5_hq0T", [128, NFC * TQ], BF16)
    w1q_b = [SB(s5, f"s5_w1q{i}", [128, NFC * 512], BF16) for i in range(2)]
    w2q_b = [SB(s5, f"s5_w2q{i}", [128, 64 * 128], BF16) for i in range(2)]
    hq1T_q = SB(s5, "s5_hq1T", [128, 64 * TQ], BF16)
    h1_b2 = [SB(s5, f"s5_h1{i}", [128, 4 * 512], BF16) for i in range(2)]
    g_b2 = [SB(s5, f"s5_g{i}", [128, 4 * 512], BF16) for i in range(2)]
    gs_b = SB(s5, "s5_gs", [128, 4 * 512], F32)
    gv_b = SB(s5, "s5_gv", [128, 4 * 512], F32)
    u5_b = SB(s5, "s5_u", [128, 4 * 512], BF16)
    hq1_b = [SB(s5, f"s5_hq1{i}", [128, 4 * 512], BF16) for i in range(3)]
    am5 = SB(s5, "s5_am", [128, 64], F32)
    sb5 = SB(s5, "s5_sb", [128, 64], I32)
    ib5 = SB(s5, "s5_ib", [128, 64], I32)
    scb5 = SB(s5, "s5_scb", [128, 64], BF16)
    ivb5 = SB(s5, "s5_ivb", [128, 64], BF16)
    h2_b = SB(s5, "s5_h2", [128, TQ], BF16)
    xbt_q = SB(s5, "s5_xbtq", [128, TQ], BF16)
    ffo_b = SB(s5, "s5_ffo", [128, TQ], BF16)
    p_m1 = [PS(s5, f"s5_pm1{i}", [128, 512], F32) for i in range(2)]
    p_t5 = [PS(s5, f"s5_pt5{i}", [128, 512], BF16) for i in range(2)]
    p_m2 = [PS(s5, f"s5_pm2{i}", [128, 512], F32) for i in range(2)]

    B1LD = None

    def emit_quant5(q, s_q):
        hqb = hq1_b[s_q % 3]
        g_b = g_b2[s_q % 2]

        def mk_q5(e, hqb=hqb, g_b=g_b):
            return quant_chain(e, g_b, hqb, am5, sb5, ib5, scb5, ivb5, 64, u5_b)
        w_q5 = [sch.cnt[("s5ge", q, s_q)]]
        te_prev = sch.cnt.get(("s5tr", q, s_q - 3, 3), None)
        if te_prev:
            w_q5.append(te_prev)
        sch.emit("vector", mk_q5, waits=w_q5, inc="eV")
        sch.cnt[("s5q5", q, s_q)] = sch.ms("eV")

    for q in range(NQ):
        w_prevq = [sch.cnt.get(("s5m1done", q - 1), None)]
        w_prevq = [w for w in w_prevq if w]

        def mk_ldh(q=q):
            def f(e):
                for c in range(NFC):
                    dq(e, "d4", hq0T_q[:, c * TQ:(c + 1) * TQ],
                       hq0T_s[c * 128:(c + 1) * 128, q * TQ:(q + 1) * TQ])
                return None
            return f
        sch.emit("sync", mk_ldh(), waits=(BAR if q == 0 else []) + w_prevq,
                 inc="d4", inc_amt=16 * NFC)
        HLD = sch.ms("d4")

        for s_ in range(NFC):
            w1b = w1q_b[s_ % 2]
            w_w1 = [sch.cnt.get(("s5w1", q, s_ - 2), None)
                    or (sch.cnt.get(("s5w1", q - 1, NFC - 2 + s_), None)
                        if q > 0 else None)]
            w_w1 = [w for w in w_w1 if w]

            w1slot = "d%d" % (s_ % 2)

            def mk_ldw1(s_=s_, w1b=w1b, half=0):
                return lambda e: e.dma_start(
                    out=w1b[:, half * 4096:(half + 1) * 4096],
                    in_=W1R_s[:, s_ * 8192 + half * 4096:
                              s_ * 8192 + (half + 1) * 4096])
            sch.emit("scalar", mk_ldw1(s_, w1b, 0),
                     waits=w_w1 + (BAR if s_ < 2 else []), inc=w1slot, inc_amt=16)
            sch.emit("sync", mk_ldw1(s_, w1b, 1),
                     waits=w_w1 + (BAR if s_ < 2 else []), inc=w1slot, inc_amt=16)
            W1LD = sch.ms(w1slot)
            b1b = b1sl_b[s_ % 2]
            w_b1 = [sch.cnt.get(("s5b1", q, s_ - 2), None)
                    or (sch.cnt.get(("s5b1", q - 1, NFC - 2 + s_), None)
                        if q > 0 else None)]
            w_b1 = [w for w in w_b1 if w]
            b1slot = "dL" if s_ % 2 == 0 else "dC"
            sch.emit("sync", (lambda b1b=b1b, s_=s_: lambda e: e.dma_start(
                out=b1b[:, :],
                in_=b1[0:1, s_ * 512:(s_ + 1) * 512]))(),
                waits=w_b1, inc=b1slot, inc_amt=16)
            B1SL = sch.ms(b1slot)

            M1last = None
            for tt_ in range(4):
                pm = p_m1[tt_ % 2]
                w_pm = [sch.cnt.get(("s5ev", q, s_, tt_ - 2), None)]
                if tt_ < 2 and s_ > 0:
                    w_pm.append(sch.cnt.get(("s5ev", q, s_ - 1, tt_ + 2), None))
                w_pm = [w for w in w_pm if w]

                def mk_m1(tt_=tt_, w1b=w1b, pm=pm, b1b=b1b):
                    def f(e):
                        e.matmul(out=pm[:, :], lhsT=onesf_t[0:1, :],
                                 rhs=b1b[0:1, :], start=True, stop=False)
                        last = None
                        for c in range(NFC):
                            last = e.matmul(
                                out=pm[:, :],
                                lhsT=hq0T_q[:, c * TQ + tt_ * 128:c * TQ + (tt_ + 1) * 128],
                                rhs=w1b[:, c * 512:(c + 1) * 512],
                                start=False, stop=(c == NFC - 1))
                        return last
                    return f
                sch.emit("tensor", mk_m1(), waits=[HLD, W1LD, B1SL] + w_pm,
                         inc="eP")
                M1 = sch.ms("eP")
                M1last = M1

                h1_b = h1_b2[s_ % 2]

                def mk_ev1(tt_=tt_, pm=pm, h1_b=h1_b):
                    return lambda e: e.copy(
                        out=h1_b[:, tt_ * 512:(tt_ + 1) * 512], in_=pm[:, :])
                w_e1 = [M1]
                if tt_ == 0:
                    ge_prev = sch.cnt.get(("s5ge", q, s_ - 2), None) or \
                              (sch.cnt.get(("s5ge", q - 1, NFC - 2 + s_), None)
                               if q > 0 and s_ < 2 else None)
                    if ge_prev:
                        w_e1.append(ge_prev)
                E1 = sch.emit("scalar", mk_ev1(), waits=w_e1, inc="eS")
                sch.cnt[("s5ev", q, s_, tt_)] = E1
            sch.cnt[("s5b1", q, s_)] = M1last
            h1_b = h1_b2[s_ % 2]
            w_sq5 = []
            gv_prev = sch.cnt.get(("s5gv", q, s_ - 1), None) or \
                      (sch.cnt.get(("s5gv", q - 1, NFC - 1), None)
                       if q > 0 and s_ == 0 else None)
            if gv_prev:
                w_sq5.append(gv_prev)
            sch.emit("scalar", (lambda h1_b=h1_b: lambda e: (
                e.drain(),
                e.activation(out=gs_b[:, :], in_=h1_b[:, :],
                             func=Act.Square))[-1])(),
                waits=w_sq5, inc="eS")
            GSQ = sch.ms("eS")
            def mk_gel1(h1_b=h1_b):
                def f(e):
                    e.tensor_scalar(out=gs_b[:, :], in0=gs_b[:, :],
                                    scalar1=0.044715, scalar2=1.0,
                                    op0=Alu.mult, op1=Alu.add)
                    e.drain()
                    return e.tensor_tensor(out=gv_b[:, :], in0=gs_b[:, :],
                                           in1=h1_b[:, :], op=Alu.mult)
                return f
            sch.emit("vector", mk_gel1(), waits=[GSQ], inc="eV")
            GV = sch.ms("eV")
            sch.cnt[("s5gv", q, s_)] = GV
            if s_ >= 1:
                emit_quant5(q, s_ - 1)
            sch.emit("scalar", (lambda: lambda e:
                     e.activation(out=gv_b[:, :], in_=gv_b[:, :], func=Act.Tanh,
                                  scale=float(0.7978845608028654)))(),
                     waits=[GV], inc="eS")
            GT = sch.ms("eS")
            def mk_gel2(h1_b=h1_b, g_b=g_b2[s_ % 2]):
                def f(e):
                    e.tensor_scalar(out=gv_b[:, :], in0=gv_b[:, :],
                                    scalar1=0.5, scalar2=0.5,
                                    op0=Alu.mult, op1=Alu.add)
                    e.drain()
                    return e.tensor_tensor(out=g_b[:, :], in0=gv_b[:, :],
                                           in1=h1_b[:, :], op=Alu.mult)
                return f
            sch.emit("vector", mk_gel2(), waits=[GT], inc="eV")
            GE = sch.ms("eV")
            sch.cnt[("s5ge", q, s_)] = GE

            if s_ == NFC - 1:
                emit_quant5(q, s_)
            sch.cnt[("s5w1", q, s_)] = sch.ms("eP")
            for s_d in ([s_ - 2] if s_ < NFC - 1 else [s_ - 2, s_ - 1, s_]):
                if s_d < 0:
                    continue
                hqd = hq1_b[s_d % 3]
                Q5d = sch.cnt[("s5q5", q, s_d)]
                for tg in range(4):
                    pt5 = p_t5[tg % 2]

                    def mk_tr5(tg=tg, hqd=hqd, pt5=pt5):
                        def f(e):
                            last = None
                            for k in range(4):
                                last = e.transpose(
                                    out=pt5[:, k * 128:(k + 1) * 128],
                                    in_=hqd[:, tg * 512 + k * 128:tg * 512 + (k + 1) * 128],
                                    identity=ident_t[:, :])
                            return last
                        return f
                    w_t5 = [Q5d]
                    pt_prev = sch.cnt.get(("s5tr", q, s_d, tg - 2), None)
                    if pt_prev:
                        w_t5.append(pt_prev)
                    elif tg < 2:
                        pr = sch.cnt.get(("s5tr", q, s_d - 1, tg + 2), None)
                        if pr:
                            w_t5.append(pr)
                    sch.emit("tensor", mk_tr5(), waits=w_t5, inc="eP")
                    TR5 = sch.ms("eP")

                    def mk_tev(s_d=s_d, tg=tg, pt5=pt5):
                        def f(e):
                            last = None
                            for k in range(4):
                                oc = s_d * 4 + k
                                last = e.copy(
                                    out=hq1T_q[:, oc * TQ + tg * 128:oc * TQ + (tg + 1) * 128],
                                    in_=pt5[:, k * 128:(k + 1) * 128])
                            return last
                        return f
                    TE = sch.emit("scalar", mk_tev(), waits=[TR5], inc="eS")
                    sch.cnt[("s5tr", q, s_d, tg)] = TE
        sch.cnt[("s5m1done", q)] = sch.ms("eP")
        M1DONE = sch.barrier()

        for hc in range(NFC):
            w2b = w2q_b[hc % 2]
            w_w2 = [sch.cnt.get(("s5w2", q, hc - 2), None)
                    or (sch.cnt.get(("s5w2", q - 1, NFC - 2 + hc), None)
                        if q > 0 else None)]
            w_w2 = [w for w in w_w2 if w]

            def mk_ldw2(hc=hc, w2b=w2b, half=0):
                return lambda e: e.dma_start(
                    out=w2b[:, half * 4096:(half + 1) * 4096],
                    in_=W2R_s[:, hc * 8192 + half * 4096:
                              hc * 8192 + (half + 1) * 4096])
            w2slot = "d%d" % (5 + hc % 2)
            w_w2b = w_w2 + ([BAR[0]] if q == 0 and hc < 2 else [])
            sch.emit("scalar", mk_ldw2(hc, w2b, 0), waits=w_w2b,
                     inc=w2slot, inc_amt=16)
            sch.emit("sync", mk_ldw2(hc, w2b, 1), waits=w_w2b,
                     inc=w2slot, inc_amt=16)
            W2LD = sch.ms(w2slot)
            def mk_ldxb(hc=hc, q=q):
                return lambda e: e.dma_start(
                    out=xbt_q[:, :],
                    in_=xbT_s[hc * 128:(hc + 1) * 128, q * TQ:(q + 1) * TQ])
            w_xb = [sch.cnt.get(("s5ff", q, hc - 1), None)]
            w_xb = [w for w in w_xb if w]
            sch.emit("sync", mk_ldxb(), waits=w_xb + (M1DONE if hc == 0 else []),
                     inc="d7", inc_amt=16)
            XBLD = sch.ms("d7")

            pm2 = p_m2[hc % 2]
            w_pm2 = [sch.cnt.get(("s5h2", q, hc - 2), None)]
            w_pm2 = [w for w in w_pm2 if w]

            def mk_m2(hc=hc, w2b=w2b, pm2=pm2):
                def f(e):
                    last = None
                    for oc in range(64):
                        last = e.matmul(
                            out=pm2[:, :],
                            lhsT=w2b[:, oc * 128:(oc + 1) * 128],
                            rhs=hq1T_q[:, oc * TQ:(oc + 1) * TQ],
                            start=(oc == 0), stop=(oc == 63))
                    return last
                return f
            sch.emit("tensor", mk_m2(), waits=[W2LD, M1DONE[5]] + w_pm2, inc="eP")
            M2 = sch.ms("eP")
            sch.cnt[("s5w2", q, hc)] = M2
            w_h2 = [M2]
            ffo_prev = sch.cnt.get(("s5ffv", q, hc - 1), None)
            if ffo_prev:
                w_h2.append(ffo_prev)
            sch.emit("scalar", (lambda hc=hc, pm2=pm2: lambda e:
                     e.activation(out=h2_b[:, :], in_=pm2[:, :], func=Act.Identity,
                                  bias=bias_b2_t[:, hc:hc + 1], scale=1.0))(),
                     waits=w_h2, inc="eS")
            sch.cnt[("s5h2", q, hc)] = sch.ms("eS")
            H2E = sch.ms("eS")

            def mk_ffo(e):
                return e.tensor_tensor(out=ffo_b[:, :], in0=h2_b[:, :],
                                       in1=xbt_q[:, :], op=Alu.add)
            w_ffo = [H2E, XBLD]
            g_prev5 = sch.cnt.get(("s5ff", q, hc - 1), None)
            if g_prev5:
                w_ffo.append(g_prev5)
            sch.emit("vector", mk_ffo, waits=w_ffo, inc="eV")
            FFO = sch.ms("eV")
            sch.cnt[("s5ffv", q, hc)] = FFO
            G = sch.emit("gpsimd", (lambda hc=hc, q=q: lambda e:
                         e.dma_start(out=ffoT_s[hc * 128:(hc + 1) * 128,
                                                q * TQ:(q + 1) * TQ],
                                     in_=ffo_b[:, :]))(),
                         waits=[FFO], inc="dS5", inc_amt=16)
            sch.cnt[("s5ff", q, hc)] = G
        sch.cnt[("s5qdone", q)] = sch.ms("dS5")
    s5.close()
    BAR = sch.barrier()

    s6 = contextlib.ExitStack()
    acT_full = SB(s6, "s6_acT", [128, NFC * S], BF16)
    wo_b = [SB(s6, f"s6_wo{i}", [128, NFC * 128], BF16) for i in range(2)]
    xt_b = [SB(s6, f"s6_xt{i}", [128, S], F32) for i in range(2)]
    ffo_t = [SB(s6, f"s6_ffo{i}", [128, S], BF16) for i in range(2)]
    e1_b = SB(s6, "s6_e1", [128, S], F32)
    out_b = [SB(s6, f"s6_out{i}", [128, S], F32) for i in range(2)]
    p_o = [PS(s6, f"s6_po{i}", [128, 2048], F32) for i in range(2)]

    def mk_ldac(e):
        for c in range(NFC):
            dq(e, "d4", acT_full[:, c * S:(c + 1) * S],
               acT_s[c * 128:(c + 1) * 128, :])
        return None
    sch.emit("sync", mk_ldac, waits=BAR, inc="d4", inc_amt=16 * NFC)
    ACLD = sch.ms("d4")

    for gc in range(NFC):
        wob = wo_b[gc % 2]
        xtb = xt_b[gc % 2]
        ffb = ffo_t[gc % 2]
        ob = out_b[gc % 2]
        w_wo = [sch.cnt.get(("s6o", gc - 2), None)]
        w_wo = [w for w in w_wo if w]

        def mk_ldwo(gc=gc, wob=wob):
            return lambda e: e.dma_start(
                out=wob[:, :],
                in_=woR[:, gc * 2048:(gc + 1) * 2048])
        woslot = "gA" if gc % 2 == 0 else "gB"
        sch.emit("gpsimd", mk_ldwo(), waits=w_wo + (BAR if gc < 2 else []),
                 inc=woslot, inc_amt=16)
        WOLD = sch.ms(woslot)

        xslot = "d%d" % (2 if gc % 2 == 0 else 7)
        def mk_ldx(gc=gc, xtb=xtb, ffb=ffb, xslot=xslot):
            def f(e):
                dq(e, xslot, xtb[:, :], xT[gc * 128:(gc + 1) * 128, :])
                dq(e, xslot, ffb[:, :], ffoT_s[gc * 128:(gc + 1) * 128, :])
                return None
            return f
        sch.emit("sync", mk_ldx(), waits=w_wo + (([ACLD] + BAR) if gc < 2 else []),
                 inc=xslot, inc_amt=32)
        XLD = sch.ms(xslot)

        po = p_o[gc % 2]
        w_po = [sch.cnt.get(("s6fin", gc - 2), None)]
        w_po = [w for w in w_po if w]

        def mk_omm(gc=gc, wob=wob, po=po):
            def f(e):
                last = None
                for c in range(NFC):
                    for j in range(4):
                        last = e.matmul(
                            out=po[:, j * 512:(j + 1) * 512],
                            lhsT=wob[:, c * 128:(c + 1) * 128],
                            rhs=acT_full[:, c * S + j * 512:c * S + (j + 1) * 512],
                            start=(c == 0), stop=(c == NFC - 1))
                return last
            return f
        OMM = sch.emit("tensor", mk_omm(), waits=[ACLD, WOLD] + w_po, inc="eP")
        sch.cnt[("s6mm", gc)] = OMM

        def mk_fin(gc=gc, po=po, xtb=xtb, ffb=ffb, ob=ob):
            def f(e):
                e.scalar_tensor_tensor(out=e1_b[:, :], in0=po[:, :],
                                       scalar=bias_o_t[:, gc:gc + 1],
                                       in1=xtb[:, :], op0=Alu.add, op1=Alu.add)
                e.drain()
                return e.tensor_tensor(out=ob[:, :], in0=e1_b[:, :],
                                       in1=ffb[:, :], op=Alu.add)
            return f
        sch.emit("vector", mk_fin(), waits=[OMM, XLD], inc="eV")
        FIN = sch.ms("eV")
        sch.cnt[("s6fin", gc)] = FIN
        oslot = "dOUTa" if gc % 2 == 0 else "dOUTb"
        G = sch.emit("gpsimd", (lambda gc=gc, ob=ob: lambda e:
                     e.dma_start(out=outT[gc * 128:(gc + 1) * 128, :], in_=ob[:, :]))(),
                     waits=[FIN], inc=oslot, inc_amt=16)
        sch.cnt[("s6o", gc)] = G
    s6.close()

    sch.emit("gpsimd", lambda e: None, waits=[sch.ms("dOUTa"), sch.ms("dOUTb")])

    with nc.Block() as block:
        @block.sync
        def _(e):
            sch.replay("sync", e, sems)

        @block.scalar
        def _(e):
            sch.replay("scalar", e, sems)

        @block.vector
        def _(e):
            sch.replay("vector", e, sems)

        @block.tensor
        def _(e):
            sch.replay("tensor", e, sems)

        @block.gpsimd
        def _(e):
            sch.replay("gpsimd", e, sems)

    ctx.close()
    return nc


_NC_CACHE = None


def kernel(**inputs):
    global _NC_CACHE
    d = inputs
    ncore = 8
    if _NC_CACHE is None:
        _NC_CACHE = build_nc()
    nc = _NC_CACHE

    ident_np = np.eye(128, dtype=BF)
    ones_np = np.ones((128, 128), dtype=BF)
    kk, qq = np.meshgrid(np.arange(128), np.arange(512), indexing="ij")
    maskslab_np = np.zeros((128, 2048), dtype=BF)
    for _t in range(4):
        maskslab_np[:, _t * 512:(_t + 1) * 512] = np.where(
            _t * 128 + kk > qq, np.float32(-1e30), np.float32(0)).astype(BF)

    W1 = np.ascontiguousarray(d["L0_ffn_W1"], dtype=np.float32)
    W2 = np.ascontiguousarray(d["L0_ffn_W2"], dtype=np.float32)
    common = dict(
        w_kv=np.ascontiguousarray(d["L0_attn_w_kv"], np.float32),
        b_kv=np.ascontiguousarray(d["L0_attn_b_kv"], np.float32).reshape(1, -1),
        w_ku=np.ascontiguousarray(d["L0_attn_w_ku"], np.float32),
        b_ku=np.ascontiguousarray(d["L0_attn_b_ku"], np.float32).reshape(1, -1),
        w_kr=np.ascontiguousarray(d["L0_attn_w_kr"], np.float32),
        b_kr=np.ascontiguousarray(d["L0_attn_b_kr"], np.float32).reshape(1, -1),
        w_vu=np.ascontiguousarray(d["L0_attn_w_vu"], np.float32),
        b_vu=np.ascontiguousarray(d["L0_attn_b_vu"], np.float32).reshape(1, -1),
        w_q=np.ascontiguousarray(d["L0_attn_w_q"], np.float32),
        b_q=np.ascontiguousarray(d["L0_attn_b_q"], np.float32).reshape(1, -1),
        w_qu=np.ascontiguousarray(d["L0_attn_w_qu"], np.float32),
        b_qu=np.ascontiguousarray(d["L0_attn_b_qu"], np.float32).reshape(1, -1),
        w_qr=np.ascontiguousarray(d["L0_attn_w_qr"], np.float32),
        b_qr=np.ascontiguousarray(d["L0_attn_b_qr"], np.float32).reshape(1, -1),
        woR=np.ascontiguousarray(
            d["L0_attn_w_o"].astype(np.float32).reshape(16, 128, 16, 128)
            .transpose(1, 2, 0, 3).reshape(128, 16 * 16 * 128)),
        b_o=np.ascontiguousarray(d["L0_attn_b_o"], np.float32).reshape(1, -1),
        ln_w=np.ascontiguousarray(d["L0_ffn_ln_w"], np.float32).reshape(1, -1),
        ln_b=np.ascontiguousarray(d["L0_ffn_ln_b"], np.float32).reshape(1, -1),
        W1=W1, W1T=np.ascontiguousarray(W1.T),
        b1=np.ascontiguousarray(d["L0_ffn_b1"], np.float32).reshape(1, -1),
        W2=W2, W2T=np.ascontiguousarray(W2.T),
        b2=np.ascontiguousarray(d["L0_ffn_b2"], np.float32).reshape(1, -1),
        ident=ident_np, ones_bf=ones_np, maskslab=maskslab_np,
        ident_f=np.eye(128, dtype=np.float32),
        ones_f=np.ones((128, 128), dtype=np.float32),
    )
    x = np.ascontiguousarray(d["L0_attn_x"], np.float32)
    in_maps = []
    for c in range(ncore):
        m = dict(common)
        m["x_nat"] = np.ascontiguousarray(x[c])
        m["xT"] = np.ascontiguousarray(x[c].T)
        in_maps.append(m)

    res = run_bass_kernel_spmd(nc, in_maps, core_ids=list(range(ncore)))
    out = np.empty((ncore, S, D), dtype=np.float32)
    for c in range(ncore):
        out[c] = res.results[c]["outT"].T
    return out
